# version 16
# baseline (speedup 1.0000x reference)
import numpy as np
from ml_dtypes import bfloat16
from contextlib import ExitStack

import concourse.bass as bass
import concourse.mybir as mybir
from concourse.bass_utils import run_bass_kernel_spmd

F32 = mybir.dt.float32
BF16 = mybir.dt.bfloat16
I32 = mybir.dt.int32
AF = mybir.ActivationFunctionType
ALU = mybir.AluOpType
AX = mybir.AxisListType

S, D, H = 2048, 1024, 16
NCH = 8
NT, TT = 4, 512
NSUB, SB = 4, 128
PI = float(np.pi)
MAGIC = float(np.float32(1.5 * 2 ** 23))
QK_RSQRT = 0x5F3759DF
INV75 = float(np.float32(1.0 / 7.5))
N_CORES = 8


class Prog:

    ENGS = ("t", "v", "s", "g", "y")

    DQS = ("y", "s", "g")

    def __init__(self, nc, es):
        self.nc = nc
        self.sem = {e: es.enter_context(nc.semaphore(f"pse_{e}")) for e in self.ENGS}
        self.dsem = {q: es.enter_context(nc.semaphore(f"psd_{q}")) for q in self.DQS}
        self.msem = {q: es.enter_context(nc.semaphore(f"psm_{q}")) for q in self.DQS}
        self.cnt = dict.fromkeys(self.ENGS, 0)
        self.dcnt = dict.fromkeys(self.DQS, 0)
        self.mcnt = dict.fromkeys(self.DQS, 0)
        self.mcov = dict.fromkeys(self.DQS, 0)
        self.queue = {e: [] for e in self.ENGS}
        self.mark = {e: {} for e in self.ENGS}
        self.wtok = {}
        self.rtok = {}

    def _emit_wait(self, eng, owner, val):
        hi = self.mark[eng].get(owner, 0)
        if val <= hi:
            return
        self.mark[eng][owner] = val
        if owner in self.ENGS:
            sem = self.sem[owner]
        elif owner[0] == "M":
            sem = self.msem[owner[1:]]
        else:
            sem = self.dsem[owner[1:]]
        self.queue[eng].append(("w", sem, val))

    def _milestone(self, q):
        if self.mcov[q] < self.dcnt[q]:
            self.mcnt[q] += 1
            self.queue[q].append(("m", self.dsem[q], 16 * self.dcnt[q],
                                  self.msem[q]))
            self.mcov[q] = self.dcnt[q]
        return self.mcnt[q]

    def _deps(self, eng, reads, writes):
        deps = []
        for tok in reads:
            d = self.wtok.get(id(tok))
            if d:
                deps.append(d)
        for tok in writes:
            d = self.wtok.get(id(tok))
            if d:
                deps.append(d)
            for owner, val in self.rtok.get(id(tok), {}).items():
                if owner == eng and eng == "t":
                    continue
                deps.append((owner, val))
        for owner, val in deps:
            if owner == "t" and eng == "t":
                continue
            if owner.startswith("D"):
                q = owner[1:]
                if eng == q:
                    val = 16 * self.dcnt[q]
                    self._emit_wait(eng, owner, val)
                    continue
                owner, val = "M" + q, self._milestone(q)
            self._emit_wait(eng, owner, val)

    def op(self, eng, fn, reads=(), writes=()):
        self.checkpoint()
        import os
        self._opidx = getattr(self, "_opidx", -1) + 1
        skips = os.environ.get("KSKIP", "")
        if skips and str(self._opidx) in skips.split(","):
            return
        self._deps(eng, reads, writes)
        self.cnt[eng] += 1
        val = self.cnt[eng]
        self.queue[eng].append(("i", fn, self.sem[eng], val))
        for tok in reads:
            self.rtok.setdefault(id(tok), {})[eng] = val
        for tok in writes:
            self.wtok[id(tok)] = (eng, val)
            self.rtok[id(tok)] = {}

    def dma(self, q, out_ap, in_ap, reads=(), writes=(), transpose=False):
        self.checkpoint()
        self._opidx = getattr(self, "_opidx", -1) + 1
        self._deps(q, reads, writes)
        self.dcnt[q] += 1
        val = 16 * self.dcnt[q]
        self.queue[q].append(("d", out_ap, in_ap, self.dsem[q], val, transpose))
        owner = "D" + q
        for tok in reads:
            self.rtok.setdefault(id(tok), {})[owner] = val
        for tok in writes:
            self.wtok[id(tok)] = (owner, val)
            self.rtok[id(tok)] = {}

    def checkpoint(self):
        if not hasattr(self, "ckpts"):
            self.ckpts = []
        self.ckpts.append(({e: len(self.queue[e]) for e in self.ENGS},
                           dict(self.dcnt)))

    def truncate(self, idx):
        lens, dc = self.ckpts[idx]
        for e in self.ENGS:
            self.queue[e] = self.queue[e][:lens[e]]
        self.dcnt = dc

    def final_store_barrier(self):
        for q in self.DQS:
            if self.dcnt[q]:
                self.queue["y"].append(("w", self.dsem[q], 16 * self.dcnt[q]))

    def run(self, block):
        def mk(eng):
            def body(h):
                for item in self.queue[eng]:
                    if item[0] == "w":
                        h.wait_ge(item[1], item[2])
                    elif item[0] == "i":
                        item[1](h).then_inc(item[2], 1)
                    elif item[0] == "m":
                        h.wait_ge(item[1], item[2]).then_inc(item[3], 1)
                    else:
                        h.dma_start(out=item[1], in_=item[2],
                                    transpose=item[5]).then_inc(item[3], 16)
            return body
        block.tensor(mk("t"))
        block.vector(mk("v"))
        block.scalar(mk("s"))
        block.gpsimd(mk("g"))
        block.sync(mk("y"))


def build_program():
    nc = bass.Bass("TRN2", target_bir_lowering=False, debug=False)
    es = ExitStack()

    xq = nc.dram_tensor("xq", [S, D], F32, kind="ExternalInput").ap()
    w_in = {nm: nc.dram_tensor(nm, [D, D], F32, kind="ExternalInput").ap()
            for nm in ("wz", "wq", "wr", "wh", "wo", "wf")}
    wphibd = nc.dram_tensor("wphibd", [128, 128], F32, kind="ExternalInput").ap()
    colpack = nc.dram_tensor("colpack", [128, 17], F32, kind="ExternalInput").ap()
    cstpack = nc.dram_tensor("cstpack", [128, 1280], BF16, kind="ExternalInput").ap()
    rowpack = nc.dram_tensor("rowpack", [6, D], F32, kind="ExternalInput").ap()
    out_d = nc.dram_tensor("out", [S, D], F32, kind="ExternalOutput").ap()
    qc_hbm = nc.dram_tensor("qc_hbm", [H, 128, S], BF16, kind="Internal").ap()
    xdq_hbm = nc.dram_tensor("xdq_hbm", [NCH, 128, S], BF16, kind="Internal").ap()

    T = {}
    def sb(name, shape, dt):
        T[name] = es.enter_context(nc.sbuf_tensor(name, list(shape), dt))
        return T[name]

    for nm in ("wzb", "wqb", "wrb", "whb"):
        sb(nm, (128, NCH * D), BF16)
    sb("wphib", (128, 128), BF16)
    sb("ident", (128, 128), BF16)

    sb("colp", (128, 17), F32)
    sb("cstb", (128, 1280), BF16)
    sb("bhb", (128, D), BF16)
    sb("bqcol", (128, 8), F32)
    sb("brcol", (128, 8), F32)
    sb("rgcol", (128, 8), F32)
    sb("cvcol", (128, 8), F32)
    sb("cvcolb", (128, 8), BF16)
    sb("bqrow", (1, D), BF16)
    sb("brrow", (1, D), BF16)
    for nm in ("bzrow", "bhrow", "borow", "bfrow"):
        sb(nm, (1, D), BF16)
    sb("pihalf", (128, 1), F32)
    sb("onesr", (1, 128), BF16)
    sb("qkcol", (128, 1), I32)

    sb("xtile", (128, NSUB * D), F32)
    sb("xst", (128, 2 * NSUB), F32)
    sb("zst", (128, 2), F32)
    sb("stat8", (128, 8), F32)
    sb("stat2", (128, 2), F32)
    for k in "abcde":
        sb("nw8" + k, (128, 16), F32)
    sb("rstd1", (128, 8), F32)
    sb("nm1", (128, NSUB), F32)
    sb("rstd2", (128, 2), F32)
    sb("nm2", (128, 1), F32)
    sb("bn6", (128, 12), F32)
    sb("xn", (128, D), F32)
    sb("xnb", (128, D), BF16)
    sb("xnT", (128, NCH * TT), BF16)
    sb("ub0", (128, D), BF16)
    sb("ub1", (128, D), BF16)
    sb("uT", (128, NCH * TT), BF16)
    sb("bmax", (128, 32), F32)
    sb("qscale", (128, 32), F32)
    sb("qrscale", (128, 32), F32)
    sb("qy", (128, D), F32)
    for i in range(4):
        sb(f"qb{i}", (128, D), BF16)
    sb("xdq", (128, D), BF16)
    T["qtt"] = T["xn"]
    T["foldrow"] = T["qy"]
    T["qk"] = T["qb3"]
    for i in (0, 1):
        sb(f"xdqs{i}", (128, NCH * 128), BF16)
    sb("psis1", (128, TT), F32)
    for i in (0, 1):
        sb(f"qTs{i}", (128, TT), BF16)
        sb(f"rTs{i}", (128, TT), BF16)
        sb(f"sinb{i}", (128, TT), BF16)
        sb(f"cosb{i}", (128, TT), BF16)
        sb(f"pqs{i}", (128, TT), BF16)
        sb(f"sqs{i}", (128, TT), BF16)
        sb(f"prs{i}", (128, TT), BF16)
        sb(f"srs{i}", (128, TT), BF16)
    sb("kcs0", (128, TT), BF16)
    T["kcs1"] = T["kcs0"]
    sb("phb0", (128, TT), F32)
    sb("psis0", (128, TT), F32)
    sb("rchv", (128, NSUB * H * 192), BF16)
    sb("gm", (128, H * 192), F32)
    sb("gmb", (128, H * 192), BF16)
    T["wob"] = T["wqb"]
    T["wdqT"] = T["wrb"]
    T["qcT"] = T["wzb"]
    T["attT"] = T["uT"]
    T["xdqT2"] = T["xnT"]
    T["prb0"] = T["sinb0"]
    T["prb1"] = T["sinb1"]
    T["rnc0"] = T["pqs0"]
    T["rnc1"] = T["pqs1"]
    T["gsb0"] = T["qb0"]
    T["gsb1"] = T["qb2"]
    T["ssb0"] = T["xdq"]
    T["ssb1"] = T["qb3"]
    T["psb0"] = T["qb1"]
    T["psb1"] = T["xnb"]
    T["outs0"] = T["qy"]
    T["outs1"] = T["qy"]

    PS = [es.enter_context(nc.psum_tensor(f"ps{i}", [128, 512], F32)) for i in range(6)]
    PG = [es.enter_context(nc.psum_tensor(f"pg{i}", [128, 512], F32)) for i in range(2)]

    XT = [object() for _ in range(NSUB)]
    p = Prog(nc, es)

    def rsqrt_newton(src_name, n, out_name, iters=2):
        t1, t2, t3, sA, sB = (T["nw8" + k] for k in "abcde")
        inn = T[src_name]
        p.op("v", lambda v: v.tensor_scalar(
            out=t1[:, 0:n].bitcast(I32), in0=inn[:, 0:n].bitcast(I32), scalar1=1,
            scalar2=None, op0=ALU.arith_shift_right), reads=[inn], writes=[t1])
        p.op("v", lambda v: v.tensor_tensor(
            out=sA[:, 0:n].bitcast(I32), in0=T["qkcol"][:].broadcast_to([128, n]),
            in1=t1[:, 0:n].bitcast(I32), op=ALU.subtract),
            reads=[t1, T["qkcol"]], writes=[sA])
        src, dst = sA, sB
        for _ in range(iters):
            p.op("v", lambda v, s_=src: v.tensor_tensor(
                out=t1[:, 0:n], in0=s_[:, 0:n], in1=s_[:, 0:n], op=ALU.mult),
                reads=[src], writes=[t1])
            p.op("v", lambda v: v.tensor_tensor(
                out=t2[:, 0:n], in0=t1[:, 0:n], in1=inn[:, 0:n], op=ALU.mult),
                reads=[t1, inn], writes=[t2])
            p.op("v", lambda v: v.tensor_scalar(
                out=t3[:, 0:n], in0=t2[:, 0:n], scalar1=-0.5, scalar2=1.5,
                op0=ALU.mult, op1=ALU.add), reads=[t2], writes=[t3])
            p.op("v", lambda v, s_=src, d_=dst: v.tensor_tensor(
                out=d_[:, 0:n], in0=s_[:, 0:n], in1=t3[:, 0:n], op=ALU.mult),
                reads=[src, t3], writes=[dst])
            src, dst = dst, src
        o = T[out_name]
        p.op("v", lambda v, s_=src: v.tensor_copy(out=o[:, 0:n], in_=s_[:, 0:n]),
             reads=[src], writes=[o])

    def rsqrt_newton_w(src_tok, src_ap, out_tok, out_ap, scr, P, n, iters=2):
        (t1k, t1), (t2k, t2), (t3k, t3), (sAk, sA), (sBk, sB) = scr
        p.op("v", lambda v: v.tensor_scalar(
            out=t1.bitcast(I32), in0=src_ap.bitcast(I32), scalar1=1,
            scalar2=None, op0=ALU.arith_shift_right), reads=[src_tok], writes=[t1k])
        p.op("v", lambda v: v.tensor_tensor(
            out=sA.bitcast(I32),
            in0=T["qkcol"][:][0:P, :].broadcast_to([P, n]),
            in1=t1.bitcast(I32), op=ALU.subtract),
            reads=[t1k, T["qkcol"]], writes=[sAk])
        src, dst = (sAk, sA), (sBk, sB)
        for _ in range(iters):
            p.op("v", lambda v, s_=src[1]: v.tensor_tensor(
                out=t1, in0=s_, in1=s_, op=ALU.mult),
                reads=[src[0]], writes=[t1k])
            p.op("v", lambda v: v.tensor_tensor(
                out=t2, in0=t1, in1=src_ap, op=ALU.mult),
                reads=[t1k, src_tok], writes=[t2k])
            p.op("v", lambda v: v.tensor_scalar(
                out=t3, in0=t2, scalar1=-0.5, scalar2=1.5,
                op0=ALU.mult, op1=ALU.add), reads=[t2k], writes=[t3k])
            p.op("v", lambda v, s_=src[1], d_=dst[1]: v.tensor_tensor(
                out=d_, in0=s_, in1=t3, op=ALU.mult),
                reads=[src[0], t3k], writes=[dst[0]])
            src, dst = dst, src
        p.op("v", lambda v, s_=src[1]: v.tensor_copy(out=out_ap, in_=s_),
             reads=[src[0]], writes=[out_tok])

    def transpose8(src_tok, src_ap_of, dst_tok, dst_ap_of, evac_eng="s"):
        for half in range(2):
            ps = PS[4 + half]
            def mk(t_, half=half, ps=ps):
                last = None
                for j in range(4):
                    last = t_.matmul(out=ps[:, j * 128:(j + 1) * 128],
                                     lhsT=src_ap_of(half * 4 + j), rhs=T["ident"][:],
                                     start=True, stop=True)
                return last
            p.op("t", mk, reads=[src_tok, T["ident"]], writes=[ps])
            out_ap = dst_ap_of(half)
            p.op(evac_eng, lambda e, o=out_ap, ps=ps: (
                e.copy(out=o, in_=ps[:].rearrange("p (c n) -> p c n", n=128))
                if evac_eng == "s" else
                e.tensor_copy(out=o, in_=ps[:].rearrange("p (c n) -> p c n", n=128))),
                reads=[ps], writes=[dst_tok])

    def quant_pipeline(src_tok, src_ap):
        s3 = src_ap.rearrange("p (b k) -> p b k", k=32)
        y2, c2, c4, t1 = T["qb0"], T["qb1"], T["qb2"], T["qb3"]
        p.op("v", lambda v: v.tensor_reduce(out=T["bmax"][:], in_=s3, axis=AX.X,
                                            op=ALU.max, apply_absolute_value=True),
             reads=[src_tok], writes=[T["bmax"]])
        p.op("v", lambda v: v.tensor_scalar(out=T["qscale"][:], in0=T["bmax"][:],
                                            scalar1=INV75, scalar2=1e-30,
                                            op0=ALU.mult, op1=ALU.max),
             reads=[T["bmax"]], writes=[T["qscale"]])
        p.op("v", lambda v: v.reciprocal(out=T["qrscale"][:], in_=T["qscale"][:]),
             reads=[T["qscale"]], writes=[T["qrscale"]])
        p.op("v", lambda v: v.tensor_tensor(
            out=T["qy"][:].rearrange("p (b k) -> p b k", k=32), in0=s3,
            in1=T["qrscale"][:].unsqueeze(2).broadcast_to([128, 32, 32]), op=ALU.mult),
            reads=[src_tok, T["qrscale"]], writes=[T["qy"]])
        p.op("s", lambda s_: s_.activation(out=y2[:], in_=T["qy"][:], func=AF.Square),
             reads=[T["qy"]], writes=[y2])
        p.op("v", lambda g: g.tensor_scalar(out=c2[:], in0=y2[:], scalar1=4.0,
                                            scalar2=None, op0=ALU.is_ge),
             reads=[y2], writes=[c2])
        p.op("v", lambda g: g.tensor_scalar(out=c4[:], in0=y2[:], scalar1=16.0,
                                            scalar2=None, op0=ALU.is_ge),
             reads=[y2], writes=[c4])
        p.op("v", lambda g: g.scalar_tensor_tensor(out=t1[:], in0=c2[:], scalar=2.0,
                                                   in1=c4[:], op0=ALU.mult, op1=ALU.add),
             reads=[c2, c4], writes=[t1])
        p.op("v", lambda g: g.tensor_scalar(out=y2[:], in0=t1[:], scalar1=-2.0,
                                            scalar2=8.0, op0=ALU.mult, op1=ALU.add),
             reads=[t1], writes=[y2])
        p.op("v", lambda g: g.scalar_tensor_tensor(out=t1[:], in0=c4[:], scalar=2.0,
                                                   in1=c2[:], op0=ALU.mult, op1=ALU.add),
             reads=[c4, c2], writes=[t1])
        p.op("v", lambda g: g.tensor_scalar(out=c2[:], in0=t1[:], scalar1=0.125,
                                            scalar2=0.125, op0=ALU.mult, op1=ALU.add),
             reads=[t1], writes=[c2])
        p.op("v", lambda v: v.tensor_tensor(out=T["qtt"][:], in0=T["qy"][:], in1=y2[:],
                                            op=ALU.mult),
             reads=[T["qy"], y2], writes=[T["qtt"]])
        p.op("v", lambda v: v.tensor_scalar(out=T["qk"][:], in0=T["qtt"][:],
                                            scalar1=MAGIC, scalar2=MAGIC,
                                            op0=ALU.add, op1=ALU.subtract),
             reads=[T["qtt"]], writes=[T["qk"]])
        p.op("v", lambda g: g.tensor_tensor(out=c4[:], in0=T["qk"][:], in1=c2[:],
                                            op=ALU.mult),
             reads=[T["qk"], c2], writes=[c4])
        p.op("v", lambda v: v.tensor_tensor(
            out=T["xdq"][:].rearrange("p (b k) -> p b k", k=32),
            in0=c4[:].rearrange("p (b k) -> p b k", k=32),
            in1=T["qscale"][:].unsqueeze(2).broadcast_to([128, 32, 32]), op=ALU.mult),
            reads=[c4, T["qscale"]], writes=[T["xdq"]])

    p.op("g", lambda g: g.memset(T["ident"][:], 0.0), writes=[T["ident"]])
    p.op("g", lambda g: g.affine_select(
        out=T["ident"][:], in_=T["ident"][:], compare_op=ALU.not_equal, fill=1.0,
        base=0, pattern=[[-1, 128]], channel_multiplier=1),
        reads=[T["ident"]], writes=[T["ident"]])
    p.op("v", lambda v: v.memset(T["onesr"][:], 1.0), writes=[T["onesr"]])
    p.op("v", lambda v: v.memset(T["qkcol"][:], QK_RSQRT), writes=[T["qkcol"]])
    p.op("v", lambda v: v.memset(T["gm"][:], 0.0), writes=[T["gm"]])

    p.dma("y", T["colp"][:], colpack[:, :], writes=[T["colp"]])
    p.dma("s", T["cstb"][:], cstpack[:, :], writes=[T["cstb"]])
    p.dma("y", T["xtile"][:, 0:128], wphibd[:, :], writes=[XT[0]])
    p.op("v", lambda v: v.tensor_copy(out=T["wphib"][:], in_=T["xtile"][:, 0:128]),
         reads=[XT[0]], writes=[T["wphib"]])
    for i, nm in ((0, "bzrow"), (1, "bqrow"), (2, "brrow"), (3, "bhrow"),
                  (4, "borow"), (5, "bfrow")):
        p.dma("y", T["xtile"][0:1, 0:D], rowpack[i:i + 1, :], writes=[XT[0]])
        p.op("v", lambda v, nm=nm: v.tensor_copy(
            out=T[nm][:], in_=T["xtile"][0:1, 0:D]),
            reads=[XT[0]], writes=[T[nm]])
    p.op("v", lambda v: v.reciprocal(out=T["rgcol"][:], in_=T["colp"][:, 0:8]),
         reads=[T["colp"]], writes=[T["rgcol"]])
    p.op("v", lambda v: v.tensor_tensor(out=T["cvcol"][:], in0=T["colp"][:, 8:16],
                                        in1=T["rgcol"][:], op=ALU.mult),
         reads=[T["colp"], T["rgcol"]], writes=[T["cvcol"]])
    p.op("v", lambda v: v.tensor_copy(out=T["cvcolb"][:], in_=T["cvcol"][:]),
         reads=[T["cvcol"]], writes=[T["cvcolb"]])
    p.op("v", lambda v: v.memset(T["pihalf"][:], PI / 2), writes=[T["pihalf"]])

    def load_weight(nm, dst, scale_by_g):
        for k in range(NCH):
            stg = T["xtile"]
            sl = k % 3
            off = sl * D
            p.dma(("y", "s", "g")[sl], stg[:, off:off + D],
                  w_in[nm][k * 128:(k + 1) * 128, :], writes=[XT[sl]])
            if scale_by_g:
                p.op("v", lambda v, k=k, off=off: v.tensor_scalar(
                    out=T[dst][:, k * D:(k + 1) * D], in0=stg[:, off:off + D],
                    scalar1=T["colp"][:, k:k + 1], scalar2=None, op0=ALU.mult),
                    reads=[XT[sl], T["colp"]], writes=[T[dst]])
            else:
                p.op("v", lambda v, k=k, off=off: v.tensor_copy(
                    out=T[dst][:, k * D:(k + 1) * D], in_=stg[:, off:off + D]),
                    reads=[XT[sl]], writes=[T[dst]])

    load_weight("wz", "wzb", False)
    load_weight("wq", "wqb", True)
    load_weight("wr", "wrb", True)
    load_weight("wh", "whb", True)

    def fold_bias(wb, rowname):
        for half in range(2):
            ps = PS[half]
            def mk(t_, half=half, ps=ps):
                last = None
                for k in range(NCH):
                    last = t_.matmul(
                        out=ps[0:1, 0:512], lhsT=T["cvcolb"][:, k:k + 1],
                        rhs=T[wb][:, k * D + half * 512: k * D + half * 512 + 512],
                        start=(k == 0), stop=(k == NCH - 1))
                return last
            p.op("t", mk, reads=[T["cvcolb"], T[wb]], writes=[ps])
        for half in range(2):
            ps = PS[half]
            p.op("v", lambda v, half=half, ps=ps: v.tensor_tensor(
                out=T["foldrow"][:][0:1, half * 512:half * 512 + 512],
                in0=ps[0:1, 0:512],
                in1=T[rowname][:][:, half * 512:half * 512 + 512],
                op=ALU.add), reads=[ps, T[rowname]], writes=[T["foldrow"]])
        p.op("v", lambda v: v.tensor_copy(out=T[rowname][:], in_=T["foldrow"][:][0:1, :]),
             reads=[T["foldrow"]], writes=[T[rowname]])

    fold_bias("wqb", "bqrow")
    fold_bias("wrb", "brrow")
    fold_bias("whb", "bhrow")

    def bias_cols(rowname, colname):
        ps = PS[2]
        def mk(t_):
            last = None
            for c in range(NCH):
                last = t_.matmul(out=ps[:, c:c + 1],
                                 lhsT=T[rowname][:][0:1, c * 128:(c + 1) * 128],
                                 rhs=T["onesr"][:][0:1, 0:1], start=True, stop=True)
            return last
        p.op("t", mk, reads=[T[rowname], T["onesr"]], writes=[ps])
        p.op("v", lambda v: v.tensor_copy(out=T[colname][:], in_=ps[:, 0:8]),
             reads=[ps], writes=[T[colname]])

    def bias_bcast(rowname, dsts):
        for half in range(2):
            ps = PS[half]
            p.op("t", lambda t_, half=half, ps=ps: t_.matmul(
                out=ps[:, 0:512], lhsT=T["onesr"][:],
                rhs=T[rowname][:][:, half * 512:half * 512 + 512],
                start=True, stop=True), reads=[T["onesr"], T[rowname]], writes=[ps])
            dtok, dap = dsts[half]
            p.op("v", lambda v, ps=ps, dap=dap: v.tensor_copy(
                out=dap, in_=ps[:, 0:512]), reads=[ps], writes=[dtok])

    bias_cols("bqrow", "bqcol")
    bias_cols("brrow", "brcol")
    bias_bcast("bhrow", [(T["bhb"], T["bhb"][:, 0:512]),
                         (T["bhb"], T["bhb"][:, 512:1024])])

    p.checkpoint()
    for t in range(NT):
        t0 = t * TT
        xt = T["xtile"]
        for sub in range(NSUB):
            p.dma("y", xt[:, sub * D:(sub + 1) * D],
                  xq[t0 + sub * SB: t0 + (sub + 1) * SB, :], writes=[XT[sub]])
        for sub in range(NSUB):
            p.op("v", lambda v, sub=sub: v.bn_stats(
                out=T["bn6"][:, 0:6], in_=xt[:, sub * D: sub * D + 512]),
                reads=[XT[sub]], writes=[T["bn6"]])
            p.op("v", lambda v, sub=sub: v.bn_stats(
                out=T["bn6"][:, 6:12], in_=xt[:, sub * D + 512: sub * D + 1024]),
                reads=[XT[sub]], writes=[T["bn6"]])
            p.op("v", lambda v, sub=sub: v.bn_aggr(
                out=T["xst"][:, 2 * sub:2 * sub + 2], in_=T["bn6"][:]),
                reads=[T["bn6"]], writes=[T["xst"]])
        p.op("v", lambda v: v.tensor_scalar(out=T["stat8"][:], in0=T["xst"][:],
                                            scalar1=1e-5, scalar2=None, op0=ALU.add),
             reads=[T["xst"]], writes=[T["stat8"]])
        rsqrt_newton("stat8", 8, "rstd1")
        for sub in range(NSUB):
            p.op("v", lambda v, sub=sub: v.scalar_tensor_tensor(
                out=T["nm1"][:, sub:sub + 1], in0=T["xst"][:, 2 * sub:2 * sub + 1],
                scalar=-1.0, in1=T["rstd1"][:, 2 * sub + 1:2 * sub + 2],
                op0=ALU.mult, op1=ALU.mult),
                reads=[T["xst"], T["rstd1"]], writes=[T["nm1"]])

        for sub in range(NSUB):
            p.op("v", lambda v, sub=sub: v.tensor_scalar(
                out=T["xn"][:], in0=xt[:, sub * D:(sub + 1) * D],
                scalar1=T["rstd1"][:, 2 * sub + 1:2 * sub + 2],
                scalar2=T["nm1"][:, sub:sub + 1], op0=ALU.mult, op1=ALU.add),
                reads=[XT[sub], T["nm1"], T["rstd1"]], writes=[T["xn"]])
            p.op("s", lambda s_: s_.copy(out=T["xnb"][:], in_=T["xn"][:]),
                 reads=[T["xn"]], writes=[T["xnb"]])
            xnT3 = T["xnT"][:].rearrange("p (c n) -> p c n", n=TT)
            transpose8(T["xnb"], lambda c: T["xnb"][:, c * 128:(c + 1) * 128],
                       T["xnT"],
                       lambda half, sub=sub: xnT3[:, half * 4:half * 4 + 4,
                                                  sub * 128:(sub + 1) * 128])
            for half in range(2):
                ps = PS[half]
                def mkz(t_, half=half, sub=sub, ps=ps):
                    last = t_.matmul(out=ps[:, 0:512], lhsT=T["onesr"][:],
                                     rhs=T["bzrow"][:][:, half * 512:half * 512 + 512],
                                     start=True, stop=False)
                    for k in range(NCH):
                        last = t_.matmul(
                            out=ps[:, 0:512],
                            lhsT=T["xnT"][:, k * TT + sub * 128: k * TT + (sub + 1) * 128],
                            rhs=T["wzb"][:, k * D + half * 512: k * D + half * 512 + 512],
                            start=False, stop=(k == NCH - 1))
                    return last
                p.op("t", mkz, reads=[T["onesr"], T["bzrow"], T["xnT"], T["wzb"]],
                     writes=[ps])
                p.op("v", lambda v, half=half, ps=ps: v.bn_stats(
                    out=T["bn6"][:, 6 * half:6 * half + 6], in_=ps[:, 0:512]),
                    reads=[ps], writes=[T["bn6"]])
            p.op("v", lambda v: v.bn_aggr(out=T["zst"][:], in_=T["bn6"][:]),
                 reads=[T["bn6"]], writes=[T["zst"]])
            p.op("v", lambda v: v.tensor_scalar(out=T["stat2"][:], in0=T["zst"][:],
                                                scalar1=1e-5, scalar2=None, op0=ALU.add),
                 reads=[T["zst"]], writes=[T["stat2"]])
            rsqrt_newton("stat2", 2, "rstd2")
            p.op("v", lambda v: v.scalar_tensor_tensor(
                out=T["nm2"][:], in0=T["zst"][:, 0:1], scalar=-1.0,
                in1=T["rstd2"][:, 1:2], op0=ALU.mult, op1=ALU.mult),
                reads=[T["zst"], T["rstd2"]], writes=[T["nm2"]])
            ubuf = T[f"ub{sub % 2}"]
            for half in range(2):
                ps = PS[half]
                p.op("v", lambda v, half=half, ubuf=ubuf, ps=ps: v.tensor_scalar(
                    out=ubuf[:, half * 512: half * 512 + 512],
                    in0=ps[:, 0:512], scalar1=T["rstd2"][:, 1:2],
                    scalar2=T["nm2"][:], op0=ALU.mult, op1=ALU.add),
                    reads=[ps, T["nm2"], T["rstd2"]], writes=[ubuf])
            uT3 = T["uT"][:].rearrange("p (c n) -> p c n", n=TT)
            transpose8(ubuf, lambda c, ubuf=ubuf: ubuf[:, c * 128:(c + 1) * 128],
                       T["uT"],
                       lambda half, sub=sub: uT3[:, half * 4:half * 4 + 4,
                                                 sub * 128:(sub + 1) * 128])
            quant_pipeline(T["xn"], T["xn"][:])
            xdqs = T[f"xdqs{sub % 2}"]
            xdqs3 = xdqs[:].rearrange("p (c n) -> p c n", n=128)
            transpose8(T["xdq"], lambda c: T["xdq"][:, c * 128:(c + 1) * 128],
                       xdqs,
                       lambda half: xdqs3[:, half * 4:half * 4 + 4, :])
            for c in range(NCH):
                p.dma("g", xdq_hbm[c, :, t0 + sub * SB: t0 + (sub + 1) * SB],
                      xdqs[:, c * 128:(c + 1) * 128], reads=[xdqs], writes=[xdq_hbm])

        for sub in range(NSUB):
            for half in range(2):
                ps = PS[half]
                def mkh(t_, half=half, sub=sub, ps=ps):
                    last = None
                    for k in range(NCH):
                        last = t_.matmul(
                            out=ps[:, 0:512],
                            lhsT=T["uT"][:, k * TT + sub * 128: k * TT + (sub + 1) * 128],
                            rhs=T["whb"][:, k * D + half * 512: k * D + half * 512 + 512],
                            start=(k == 0), stop=(k == NCH - 1))
                    return last
                p.op("t", mkh, reads=[T["uT"], T["whb"]], writes=[ps])
                base0 = sub * H * 192 + half * 8 * 192
                src3 = ps[:, 0:512].rearrange("p (h x) -> p h x", x=64)
                bhb3 = T["bhb"][:, half * 512:half * 512 + 512].rearrange(
                    "p (h x) -> p h x", x=64)
                dstt = T["rchv"][:, base0: base0 + 192 * 8]
                dst3 = dstt.rearrange("p (h x) -> p h x", x=192)[:, :, 128:192]
                p.op("v", lambda v, dst3=dst3, src3=src3, bhb3=bhb3: v.tensor_tensor(
                    out=dst3, in0=src3, in1=bhb3, op=ALU.add),
                    reads=[ps, T["bhb"]], writes=[T["rchv"]])

        for c in range(NCH):
            bi = c % 2
            qstage, rstage = T[f"qTs{bi}"], T[f"rTs{bi}"]
            kcs, psis = T[f"kcs{bi}"], T[f"psis{bi}"]
            sinb, cosb = T[f"sinb{bi}"], T[f"cosb{bi}"]
            for (wname, bcol, stage) in (("wqb", "bqcol", qstage), ("wrb", "brcol", rstage)):
                ps = PS[bi]
                def mkq(t_, wname=wname, c=c, ps=ps):
                    last = None
                    for k in range(NCH):
                        last = t_.matmul(
                            out=ps[:, 0:512],
                            lhsT=T[wname][:, k * D + c * 128: k * D + (c + 1) * 128],
                            rhs=T["uT"][:, k * TT:(k + 1) * TT],
                            start=(k == 0), stop=(k == NCH - 1))
                    return last
                p.op("t", mkq, reads=[T[wname], T["uT"]], writes=[ps])
                p.op("v", lambda v, ps=ps, stage=stage, bcol=bcol, c=c: v.tensor_scalar(
                    out=stage[:], in0=ps[:, 0:512], scalar1=T[bcol][:, c:c + 1],
                    scalar2=None, op0=ALU.add),
                    reads=[ps, T[bcol]], writes=[stage])
            phb = T["phb0"]
            for side, (stage, pbuf, sbuf_) in enumerate(
                    ((qstage, T[f"pqs{bi}"], T[f"sqs{bi}"]),
                     (rstage, T[f"prs{bi}"], T[f"srs{bi}"]))):
                ps = PS[2 + bi]
                p.op("t", lambda t_, stage=stage, ps=ps: t_.matmul(
                    out=ps[:, 0:512], lhsT=T["wphib"][:], rhs=stage[:],
                    start=True, stop=True),
                    reads=[T["wphib"], stage], writes=[ps])
                p.op("v", lambda v, ps=ps, phb=phb: v.tensor_scalar(
                    out=phb[:], in0=ps[:, 0:512], scalar1=T["colp"][:, 16:17],
                    scalar2=None, op0=ALU.add),
                    reads=[ps, T["colp"]], writes=[phb])
                p.op("v", lambda g, phb=phb, kcs=kcs: g.tensor_scalar(
                    out=kcs[:], in0=phb[:], scalar1=PI / 2, scalar2=None,
                    op0=ALU.is_ge),
                    reads=[phb], writes=[kcs])
                p.op("v", lambda v, phb=phb, kcs=kcs, psis=psis: v.scalar_tensor_tensor(
                    out=psis[:], in0=kcs[:], scalar=-2 * PI, in1=phb[:],
                    op0=ALU.mult, op1=ALU.add),
                    reads=[kcs, phb], writes=[psis])
                p.op("s", lambda s_, phb=phb, sinb=sinb: s_.activation(
                    out=sinb[:], in_=phb[:], func=AF.Sin),
                    reads=[phb], writes=[sinb])
                p.op("s", lambda s_, psis=psis, cosb=cosb: s_.activation(
                    out=cosb[:], in_=psis[:], func=AF.Sin, bias=T["pihalf"][:]),
                    reads=[psis, T["pihalf"]], writes=[cosb])
                eng = "v" if side == 0 else "g"
                p.op(eng, lambda e, stage=stage, cosb=cosb, pbuf=pbuf: e.tensor_tensor(
                    out=pbuf[:], in0=stage[:], in1=cosb[:], op=ALU.mult),
                    reads=[stage, cosb], writes=[pbuf])
                p.op(eng, lambda e, stage=stage, sinb=sinb, sbuf_=sbuf_: e.tensor_tensor(
                    out=sbuf_[:], in0=stage[:], in1=sinb[:], op=ALU.mult),
                    reads=[stage, sinb], writes=[sbuf_])
            pq, sq = T[f"pqs{bi}"], T[f"sqs{bi}"]
            for (buf, r0) in ((pq, 0), (sq, 64)):
                p.dma("g", qc_hbm[2 * c, r0:r0 + 64, t0:t0 + TT], buf[0:64, :],
                      reads=[buf], writes=[qc_hbm])
                p.dma("g", qc_hbm[2 * c + 1, r0:r0 + 64, t0:t0 + TT], buf[64:128, :],
                      reads=[buf], writes=[qc_hbm])
            pr, sr = T[f"prs{bi}"], T[f"srs{bi}"]
            for bidx, (buf, col0) in enumerate(((pr, 0), (sr, 64))):
                ps = PS[4 + bidx]
                def mkt4(t_, buf=buf, ps=ps):
                    last = None
                    for sub in range(NSUB):
                        last = t_.matmul(
                            out=ps[:, sub * 128:(sub + 1) * 128],
                            lhsT=buf[:, sub * 128:(sub + 1) * 128],
                            rhs=T["ident"][:], start=True, stop=True)
                    return last
                p.op("t", mkt4, reads=[buf, T["ident"]], writes=[ps])
                src4 = ps[:].rearrange("p (s h x) -> p s h x", s=NSUB, h=2)
                base = (2 * c) * 192 + col0
                dst4 = T["rchv"][:, base: base + (NSUB - 1) * H * 192 + 192 + 64].rearrange(
                    "p (rest) -> p rest") if False else None
                dst4 = T["rchv"][:].rearrange(
                    "p (s hh x) -> p s hh x", s=NSUB, x=192)[
                    :, :, 2 * c:2 * c + 2, col0:col0 + 64]
                p.op("s", lambda s_, src4=src4, dst4=dst4: s_.copy(
                    out=dst4, in_=src4),
                    reads=[ps], writes=[T["rchv"]])
            pg = PG[c % 2]
            def mkgm(t_, c=c, pg=pg):
                last = None
                for hh in range(2):
                    h = 2 * c + hh
                    for sub in range(NSUB):
                        base = sub * H * 192 + h * 192
                        last = t_.matmul(
                            out=pg[:, hh * 192:(hh + 1) * 192],
                            lhsT=T["rchv"][:, base:base + 128],
                            rhs=T["rchv"][:, base:base + 192],
                            start=(sub == 0), stop=(sub == NSUB - 1))
                return last
            p.op("t", mkgm, reads=[T["rchv"]], writes=[pg])
            p.op("v", lambda v, c=c, pg=pg: v.tensor_tensor(
                out=T["gm"][:, c * 384:(c + 1) * 384],
                in0=T["gm"][:, c * 384:(c + 1) * 384], in1=pg[:, 0:384], op=ALU.add),
                reads=[pg, T["gm"]], writes=[T["gm"]])


        p.checkpoint()
    p.op("s", lambda s_: s_.copy(out=T["gmb"][:], in_=T["gm"][:]),
         reads=[T["gm"]], writes=[T["gmb"]])
    bias_bcast("bfrow", [(T["qTs0"], T["qTs0"][:]), (T["qTs1"], T["qTs1"][:])])
    bias_bcast("borow", [(T["rTs0"], T["rTs0"][:]), (T["rTs1"], T["rTs1"][:])])
    load_weight("wo", "wob", False)
    for c in range(NCH):
        stg = T["xtile"]
        off = (c % 2) * D
        p.dma(("y", "s")[c % 2], stg[:, off:off + D],
              w_in["wf"][c * 128:(c + 1) * 128, :], writes=[XT[c % 2]])
        quant_pipeline(XT[c % 2], stg[:, off:off + D])
        wdqT3 = T["wdqT"][:].rearrange("p (k n) -> p k n", n=D)
        transpose8(T["xdq"], lambda k: T["xdq"][:, k * 128:(k + 1) * 128],
                   T["wdqT"],
                   lambda half, c=c: wdqT3[:, half * 4:half * 4 + 4,
                                           c * 128:(c + 1) * 128])

    p.checkpoint()
    GS = [(T["gsb0"][:], T["gsb0"]), (T["gsb1"][:], T["gsb1"]),
          (T["xdqs0"][:][:, 0:D], T["xdqs0"]), (T["xdqs1"][:][:, 0:D], T["xdqs1"])]
    SS = [(T["ssb0"][:], T["ssb0"]), (T["ssb1"][:], T["ssb1"])]
    PSB = [(T["psb0"][:], T["psb0"]), (T["psb1"][:], T["psb1"])]
    XNS = [(T["xn"][:], T["xn"]), (T["gm"][:][:, 0:D], T["gm"])]
    for t in range(NT):
        t0 = t * TT
        if t % 2 == 0:
            qct_tok, qct = T["qcT"], T["qcT"][:]
            xdt_tok, xdt = T["xdqT2"], T["xdqT2"][:]
        else:
            qct_tok, qct = T["rchv"], T["rchv"][:][:, 0:H * TT]
            xdt_tok, xdt = T["rchv"], T["rchv"][:][:, H * TT:H * TT + NCH * TT]
        for h in range(H):
            p.dma(("y", "g", "s")[h % 3], qct[:, h * TT:(h + 1) * TT],
                  qc_hbm[h, :, t0:t0 + TT], reads=[qc_hbm], writes=[qct_tok])
        for c in range(NCH):
            p.dma(("g", "s", "y")[c % 3], xdt[:, c * TT:(c + 1) * TT],
                  xdq_hbm[c, :, t0:t0 + TT], reads=[xdq_hbm], writes=[xdt_tok])
        for sub in range(NSUB):
            p.dma(("y", "g")[sub % 2], T["xtile"][:, sub * D:(sub + 1) * D],
                  xq[t0 + sub * SB: t0 + (sub + 1) * SB, :], writes=[XT[sub]])
        def emit_A(h):
            pg = PG[h % 2]
            goff = (h // 2) * 384 + (h % 2) * 192
            p.op("t", lambda t_, pg=pg, goff=goff, h=h, qct=qct: t_.matmul(
                out=pg[:, 0:512], lhsT=T["gmb"][:, goff:goff + 128],
                rhs=qct[:, h * TT:(h + 1) * TT], start=True, stop=True,
                skip_group_check=True),
                reads=[T["gmb"], qct_tok], writes=[pg])

        def emit_prod_n2(h):
            pg = PG[h % 2]
            pr = T[f"prb{h % 2}"]
            p.op("v", lambda v, pg=pg, pr=pr, h=h, qct=qct: v.tensor_tensor(
                out=pr[:], in0=pg[:, 0:512], in1=qct[:, h * TT:(h + 1) * TT],
                op=ALU.mult), reads=[pg, qct_tok], writes=[pr])
            p.op("t", lambda t_, pr=pr, h=h: t_.matmul(
                out=PS[3][0:16, 0:512],
                lhsT=T["cstb"][:, 1024 + h * 16:1024 + (h + 1) * 16], rhs=pr[:],
                start=(h == 0), stop=(h == H - 1), skip_group_check=True),
                reads=[T["cstb"], pr, PS[3]], writes=[PS[3]])

        emit_A(0)
        for h in range(1, H):
            emit_A(h)
            emit_prod_n2(h - 1)
        emit_prod_n2(H - 1)

        def ffn_sub(sub):
            xnap, xntok = XNS[sub % 2]
            gsap, gstok = GS[sub]
            for half in range(2):
                ps = PS[1 if half == 0 else 2]
                def mkg(t_, half=half, sub=sub, ps=ps, xdt=xdt):
                    last = None
                    for k in range(NCH):
                        last = t_.matmul(
                            out=ps[:, 0:512],
                            lhsT=xdt[:, k * TT + sub * 128: k * TT + (sub + 1) * 128],
                            rhs=T["wdqT"][:, k * D + half * 512: k * D + half * 512 + 512],
                            start=(k == 0), stop=(k == NCH - 1), skip_group_check=True)
                    return last
                p.op("t", mkg, reads=[xdt_tok, T["wdqT"]], writes=[ps])
                p.op("v", lambda v, ps=ps, half=half, xnap=xnap: v.tensor_tensor(
                    out=xnap[:, half * 512:half * 512 + 512], in0=ps[:, 0:512],
                    in1=T[f"qTs{half}"][:], op=ALU.add),
                    reads=[ps, T[f"qTs{half}"]], writes=[xntok])
                p.op("s", lambda s_, half=half, xnap=xnap, gsap=gsap: s_.activation(
                    out=gsap[:, half * 512:half * 512 + 512],
                    in_=xnap[:, half * 512:half * 512 + 512], func=AF.Silu),
                    reads=[xntok], writes=[gstok])

        ffn_sub(0)
        ffn_sub(1)

        p.op("v", lambda v: v.tensor_scalar(
            out=T["psis0"][0:16, 0:512], in0=PS[3][0:16, 0:512], scalar1=1e-24,
            scalar2=None, op0=ALU.add), reads=[PS[3]], writes=[T["psis0"]])
        scr16 = [(T["gm"], T["gm"][0:16, D + i * 512: D + (i + 1) * 512])
                 for i in range(4)] + [(T["psis1"], T["psis1"][0:16, 0:512])]
        rsqrt_newton_w(T["psis0"], T["psis0"][0:16, 0:512],
                       T["cosb0"], T["cosb0"][0:16, 0:512], scr16, 16, 512)

        for c in range(NCH):
            psR = PS[4 + (c % 2)]
            p.op("t", lambda t_, c=c, psR=psR: t_.matmul(
                out=psR[:, 0:512], lhsT=T["cstb"][0:16, c * 128:(c + 1) * 128],
                rhs=T["cosb0"][0:16, 0:512], start=True, stop=True,
                skip_group_check=True),
                reads=[T["cstb"], T["cosb0"]], writes=[psR])
            rnc = T[f"rnc{c % 2}"]
            p.op("s", lambda s_, psR=psR, rnc=rnc: s_.copy(
                out=rnc[:], in_=psR[:, 0:512]), reads=[psR], writes=[rnc])
            pa = PG[c % 2]
            def mkat(t_, c=c, pa=pa, qct=qct):
                last = None
                for hh in range(2):
                    h = 2 * c + hh
                    goff = (h // 2) * 384 + (h % 2) * 192 + 128
                    last = t_.matmul(out=pa[hh * 64:(hh + 1) * 64, 0:512],
                                     lhsT=T["gmb"][:, goff:goff + 64],
                                     rhs=qct[:, h * TT:(h + 1) * TT],
                                     start=True, stop=True, skip_group_check=True)
                return last
            p.op("t", mkat, reads=[T["gmb"], qct_tok], writes=[pa])
            p.op("v", lambda v, c=c, pa=pa, rnc=rnc: v.tensor_tensor(
                out=T["attT"][:, c * TT:(c + 1) * TT], in0=pa[:, 0:512], in1=rnc[:],
                op=ALU.mult), reads=[pa, rnc], writes=[T["attT"]])

        ffn_sub(2)
        ffn_sub(3)

        for sub in range(NSUB):
            ssap, sstok = SS[sub % 2]
            for half in range(2):
                ps = PS[5 if half == 0 else 0]
                def mks(t_, half=half, sub=sub, ps=ps):
                    last = None
                    for k in range(NCH):
                        last = t_.matmul(
                            out=ps[:, 0:512],
                            lhsT=T["attT"][:, k * TT + sub * 128: k * TT + (sub + 1) * 128],
                            rhs=T["wob"][:, k * D + half * 512: k * D + half * 512 + 512],
                            start=(k == 0), stop=(k == NCH - 1), skip_group_check=True)
                    return last
                p.op("t", mks, reads=[T["attT"], T["wob"]], writes=[ps])
                p.op("v", lambda v, ps=ps, half=half, ssap=ssap: v.tensor_tensor(
                    out=ssap[:, half * 512:half * 512 + 512], in0=ps[:, 0:512],
                    in1=T[f"rTs{half}"][:], op=ALU.add),
                    reads=[ps, T[f"rTs{half}"]], writes=[sstok])
            outb = T[f"outs{sub % 2}"]
            psap, pstok = PSB[sub % 2]
            gsap, gstok = GS[sub]
            p.op("v", lambda v, ssap=ssap, gsap=gsap, psap=psap: v.tensor_tensor(
                out=psap[:], in0=ssap[:], in1=gsap[:], op=ALU.mult),
                reads=[sstok, gstok], writes=[pstok])
            p.op("v", lambda v, sub=sub, outb=outb, psap=psap: v.tensor_tensor(
                out=outb[:], in0=T["xtile"][:, sub * D:(sub + 1) * D], in1=psap[:],
                op=ALU.add),
                reads=[XT[sub], pstok], writes=[outb])
            p.dma("s", out_d[t0 + sub * SB: t0 + (sub + 1) * SB, :], outb[:],
                  reads=[outb])

        p.checkpoint()

    import os
    lim = os.environ.get("KLIMIT")
    if lim is not None:
        p.truncate(int(lim))
    p.final_store_barrier()

    with nc.Block() as block:
        p.run(block)
    es.close()
    return nc


_NC_CACHE = None


def kernel(**inputs):
    global _NC_CACHE
    x = np.ascontiguousarray(np.asarray(inputs["L0_attn_query"], dtype=np.float32))
    w = {"wz": inputs["L0_attn_Wz"], "wq": inputs["L0_attn_Wq"],
         "wr": inputs["L0_attn_Wr"], "wh": inputs["L0_attn_Wh"],
         "wo": inputs["L0_attn_Wo"], "wf": inputs["L0_ffn_weight"]}
    w = {k: np.ascontiguousarray(np.asarray(v, dtype=np.float32)) for k, v in w.items()}
    wphi = np.asarray(inputs["L0_attn_Wphi"], dtype=np.float32)
    wphibd = np.zeros((128, 128), dtype=np.float32)
    wphibd[0:64, 0:64] = wphi
    wphibd[64:128, 64:128] = wphi
    colpack = np.zeros((128, 17), dtype=np.float32)
    colpack[:, 0:8] = np.asarray(inputs["L0_attn_ln_g"], np.float32).reshape(8, 128).T
    colpack[:, 8:16] = np.asarray(inputs["L0_attn_ln_b"], np.float32).reshape(8, 128).T
    colpack[0:64, 16] = np.asarray(inputs["L0_attn_bphi"], np.float32)
    colpack[64:128, 16] = np.asarray(inputs["L0_attn_bphi"], np.float32)
    cstpack = np.zeros((128, 1280), dtype=bfloat16)
    hh = np.arange(1024) // 64
    cstpack[0:16, 0:1024] = (hh[None, :] == np.arange(16)[:, None]).astype(bfloat16)
    jj = np.arange(256)
    cstpack[:, 1024:1280] = np.broadcast_to(
        ((jj % 16) == (jj // 16)).astype(bfloat16)[None, :], (128, 256))
    rowpack = np.ascontiguousarray(np.stack([
        np.asarray(inputs["L0_attn_bz"], np.float32),
        np.asarray(inputs["L0_attn_bq"], np.float32),
        np.asarray(inputs["L0_attn_br"], np.float32),
        np.asarray(inputs["L0_attn_bh"], np.float32),
        np.asarray(inputs["L0_attn_bo"], np.float32),
        np.asarray(inputs["L0_ffn_bias"], np.float32)]))
    if _NC_CACHE is None:
        _NC_CACHE = build_program()
    in_maps = []
    for b in range(N_CORES):
        m = {"xq": np.ascontiguousarray(x[b]), "wphibd": wphibd,
             "colpack": colpack, "rowpack": rowpack, "cstpack": cstpack}
        m.update(w)
        in_maps.append(m)
    res = run_bass_kernel_spmd(_NC_CACHE, in_maps, core_ids=list(range(N_CORES)))
    return np.stack([r["out"] for r in res.results]).astype(np.float32)


if __name__ == "__main__":
    pass


# version 17 (final)
# speedup vs baseline: 1.1213x; 1.1213x over previous
import numpy as np
from ml_dtypes import bfloat16
from contextlib import ExitStack

import concourse.bass as bass
import concourse.mybir as mybir
from concourse.bass_utils import run_bass_kernel_spmd

F32 = mybir.dt.float32
BF16 = mybir.dt.bfloat16
I32 = mybir.dt.int32
AF = mybir.ActivationFunctionType
ALU = mybir.AluOpType
AX = mybir.AxisListType

S, D, H = 2048, 1024, 16
NCH = 8
NT, TT = 4, 512
NSUB, SB = 4, 128
PI = float(np.pi)
MAGIC = float(np.float32(1.5 * 2 ** 23))
QK_RSQRT = 0x5F3759DF
INV75 = float(np.float32(1.0 / 7.5))
N_CORES = 8


class Prog:

    ENGS = ("t", "v", "s", "g", "y")

    DQS = ("y", "s", "g")

    def __init__(self, nc, es):
        self.nc = nc
        self.sem = {e: es.enter_context(nc.semaphore(f"pse_{e}")) for e in self.ENGS}
        self.dsem = {q: es.enter_context(nc.semaphore(f"psd_{q}")) for q in self.DQS}
        self.msem = {q: es.enter_context(nc.semaphore(f"psm_{q}")) for q in self.DQS}
        self.cnt = dict.fromkeys(self.ENGS, 0)
        self.dcnt = dict.fromkeys(self.DQS, 0)
        self.mcnt = dict.fromkeys(self.DQS, 0)
        self.mcov = dict.fromkeys(self.DQS, 0)
        self.queue = {e: [] for e in self.ENGS}
        self.mark = {e: {} for e in self.ENGS}
        self.wtok = {}
        self.rtok = {}

    def _emit_wait(self, eng, owner, val):
        hi = self.mark[eng].get(owner, 0)
        if val <= hi:
            return
        self.mark[eng][owner] = val
        if owner in self.ENGS:
            sem = self.sem[owner]
        elif owner[0] == "M":
            sem = self.msem[owner[1:]]
        else:
            sem = self.dsem[owner[1:]]
        self.queue[eng].append(("w", sem, val))

    def _milestone(self, q):
        if self.mcov[q] < self.dcnt[q]:
            self.mcnt[q] += 1
            self.queue[q].append(("m", self.dsem[q], 16 * self.dcnt[q],
                                  self.msem[q]))
            self.mcov[q] = self.dcnt[q]
        return self.mcnt[q]

    def _deps(self, eng, reads, writes):
        deps = []
        for tok in reads:
            d = self.wtok.get(id(tok))
            if d:
                deps.append(d)
        for tok in writes:
            d = self.wtok.get(id(tok))
            if d:
                deps.append(d)
            for owner, val in self.rtok.get(id(tok), {}).items():
                if owner == eng and eng == "t":
                    continue
                deps.append((owner, val))
        for owner, val in deps:
            if owner == "t" and eng == "t":
                continue
            if owner.startswith("D"):
                q = owner[1:]
                if eng == q:
                    val = 16 * self.dcnt[q]
                    self._emit_wait(eng, owner, val)
                    continue
                owner, val = "M" + q, self._milestone(q)
            self._emit_wait(eng, owner, val)

    def op(self, eng, fn, reads=(), writes=()):
        self.checkpoint()
        import os
        self._opidx = getattr(self, "_opidx", -1) + 1
        skips = os.environ.get("KSKIP", "")
        if skips and str(self._opidx) in skips.split(","):
            return
        self._deps(eng, reads, writes)
        self.cnt[eng] += 1
        val = self.cnt[eng]
        self.queue[eng].append(("i", fn, self.sem[eng], val))
        for tok in reads:
            self.rtok.setdefault(id(tok), {})[eng] = val
        for tok in writes:
            self.wtok[id(tok)] = (eng, val)
            self.rtok[id(tok)] = {}

    def dma(self, q, out_ap, in_ap, reads=(), writes=(), transpose=False):
        self.checkpoint()
        self._opidx = getattr(self, "_opidx", -1) + 1
        self._deps(q, reads, writes)
        self.dcnt[q] += 1
        val = 16 * self.dcnt[q]
        self.queue[q].append(("d", out_ap, in_ap, self.dsem[q], val, transpose))
        owner = "D" + q
        for tok in reads:
            self.rtok.setdefault(id(tok), {})[owner] = val
        for tok in writes:
            self.wtok[id(tok)] = (owner, val)
            self.rtok[id(tok)] = {}

    def checkpoint(self):
        if not hasattr(self, "ckpts"):
            self.ckpts = []
        self.ckpts.append(({e: len(self.queue[e]) for e in self.ENGS},
                           dict(self.dcnt)))

    def truncate(self, idx):
        lens, dc = self.ckpts[idx]
        for e in self.ENGS:
            self.queue[e] = self.queue[e][:lens[e]]
        self.dcnt = dc

    def final_store_barrier(self):
        for q in self.DQS:
            if self.dcnt[q]:
                self.queue["y"].append(("w", self.dsem[q], 16 * self.dcnt[q]))

    def run(self, block):
        def mk(eng):
            def body(h):
                for item in self.queue[eng]:
                    if item[0] == "w":
                        h.wait_ge(item[1], item[2])
                    elif item[0] == "i":
                        item[1](h).then_inc(item[2], 1)
                    elif item[0] == "m":
                        h.wait_ge(item[1], item[2]).then_inc(item[3], 1)
                    else:
                        h.dma_start(out=item[1], in_=item[2],
                                    transpose=item[5]).then_inc(item[3], 16)
            return body
        block.tensor(mk("t"))
        block.vector(mk("v"))
        block.scalar(mk("s"))
        block.gpsimd(mk("g"))
        block.sync(mk("y"))


def build_program():
    nc = bass.Bass("TRN2", target_bir_lowering=False, debug=False)
    es = ExitStack()

    xq = nc.dram_tensor("xq", [S, D], F32, kind="ExternalInput").ap()
    w_in = {nm: nc.dram_tensor(nm, [D, D], F32, kind="ExternalInput").ap()
            for nm in ("wz", "wq", "wr", "wh", "wo", "wf")}
    wphibd = nc.dram_tensor("wphibd", [128, 128], F32, kind="ExternalInput").ap()
    colpack = nc.dram_tensor("colpack", [128, 17], F32, kind="ExternalInput").ap()
    cstpack = nc.dram_tensor("cstpack", [128, 1280], BF16, kind="ExternalInput").ap()
    rowpack = nc.dram_tensor("rowpack", [6, D], F32, kind="ExternalInput").ap()
    out_d = nc.dram_tensor("out", [S, D], F32, kind="ExternalOutput").ap()
    qc_hbm = nc.dram_tensor("qc_hbm", [H, 128, S], BF16, kind="Internal").ap()
    xdq_hbm = nc.dram_tensor("xdq_hbm", [NCH, 128, S], BF16, kind="Internal").ap()

    T = {}
    def sb(name, shape, dt):
        T[name] = es.enter_context(nc.sbuf_tensor(name, list(shape), dt))
        return T[name]

    for nm in ("wzb", "wqb", "wrb", "whb"):
        sb(nm, (128, NCH * D), BF16)
    sb("wphib", (128, 128), BF16)
    sb("ident", (128, 128), BF16)

    sb("colp", (128, 17), F32)
    sb("cstb", (128, 1280), BF16)
    sb("bhb", (128, D), BF16)
    sb("bqcol", (128, 8), F32)
    sb("brcol", (128, 8), F32)
    sb("rgcol", (128, 8), F32)
    sb("cvcol", (128, 8), F32)
    sb("cvcolb", (128, 8), BF16)
    sb("bqrow", (1, D), BF16)
    sb("brrow", (1, D), BF16)
    for nm in ("bzrow", "bhrow", "borow", "bfrow"):
        sb(nm, (1, D), BF16)
    sb("pihalf", (128, 1), F32)
    sb("onesr", (1, 128), BF16)
    sb("qkcol", (128, 1), I32)

    sb("xtile", (128, NSUB * D), F32)
    sb("xst", (128, 2 * NSUB), F32)
    sb("zst", (128, 2), F32)
    sb("stat8", (128, 8), F32)
    sb("stat2", (128, 2), F32)
    for k in "abcde":
        sb("nw8" + k, (128, 16), F32)
    sb("rstd1", (128, 8), F32)
    sb("nm1", (128, NSUB), F32)
    sb("rstd2", (128, 2), F32)
    sb("nm2", (128, 1), F32)
    sb("bn6", (128, 12), F32)
    sb("xn", (128, D), F32)
    sb("xnb", (128, D), BF16)
    sb("xnT", (128, NCH * TT), BF16)
    sb("ub0", (128, D), BF16)
    sb("ub1", (128, D), BF16)
    sb("uT", (128, NCH * TT), BF16)
    sb("bmax", (128, 32), F32)
    sb("qscale", (128, 32), F32)
    sb("qrscale", (128, 32), F32)
    sb("qy", (128, D), F32)
    for i in range(4):
        sb(f"qb{i}", (128, D), BF16)
    sb("xdq", (128, D), BF16)
    T["qtt"] = T["xn"]
    T["foldrow"] = T["qy"]
    T["qk"] = T["qb3"]
    for i in (0, 1):
        sb(f"xdqs{i}", (128, NCH * 128), BF16)
    sb("psis1", (128, TT), F32)
    for i in (0, 1):
        sb(f"qTs{i}", (128, TT), BF16)
        sb(f"rTs{i}", (128, TT), BF16)
        sb(f"sinb{i}", (128, TT), BF16)
        sb(f"cosb{i}", (128, TT), BF16)
        sb(f"pqs{i}", (128, TT), BF16)
        sb(f"sqs{i}", (128, TT), BF16)
        sb(f"prs{i}", (128, TT), BF16)
        sb(f"srs{i}", (128, TT), BF16)
    sb("kcs0", (128, TT), BF16)
    T["kcs1"] = T["kcs0"]
    sb("phb0", (128, TT), F32)
    sb("psis0", (128, TT), F32)
    sb("rchv", (128, NSUB * H * 192), BF16)
    sb("gm", (128, H * 192), F32)
    sb("gmb", (128, H * 192), BF16)
    T["wob"] = T["wqb"]
    T["wdqT"] = T["wrb"]
    T["qcT"] = T["wzb"]
    T["attT"] = T["uT"]
    T["xdqT2"] = T["xnT"]
    T["prb0"] = T["sinb0"]
    T["prb1"] = T["sinb1"]
    T["rnc0"] = T["pqs0"]
    T["rnc1"] = T["pqs1"]
    T["gsb0"] = T["qb0"]
    T["gsb1"] = T["qb2"]
    T["ssb0"] = T["xdq"]
    T["ssb1"] = T["qb3"]
    T["psb0"] = T["qb1"]
    T["psb1"] = T["xnb"]
    T["outs0"] = T["qy"]
    T["outs1"] = T["qy"]

    PS = [es.enter_context(nc.psum_tensor(f"ps{i}", [128, 512], F32)) for i in range(6)]
    PG = [es.enter_context(nc.psum_tensor(f"pg{i}", [128, 512], F32)) for i in range(2)]

    XT = [object() for _ in range(NSUB)]
    p = Prog(nc, es)

    def rsqrt_newton(src_name, n, out_name, iters=2):
        t1, t2, t3, sA, sB = (T["nw8" + k] for k in "abcde")
        inn = T[src_name]
        p.op("v", lambda v: v.tensor_scalar(
            out=t1[:, 0:n].bitcast(I32), in0=inn[:, 0:n].bitcast(I32), scalar1=1,
            scalar2=None, op0=ALU.arith_shift_right), reads=[inn], writes=[t1])
        p.op("v", lambda v: v.tensor_tensor(
            out=sA[:, 0:n].bitcast(I32), in0=T["qkcol"][:].broadcast_to([128, n]),
            in1=t1[:, 0:n].bitcast(I32), op=ALU.subtract),
            reads=[t1, T["qkcol"]], writes=[sA])
        src, dst = sA, sB
        for _ in range(iters):
            p.op("v", lambda v, s_=src: v.tensor_tensor(
                out=t1[:, 0:n], in0=s_[:, 0:n], in1=s_[:, 0:n], op=ALU.mult),
                reads=[src], writes=[t1])
            p.op("v", lambda v: v.tensor_tensor(
                out=t2[:, 0:n], in0=t1[:, 0:n], in1=inn[:, 0:n], op=ALU.mult),
                reads=[t1, inn], writes=[t2])
            p.op("v", lambda v: v.tensor_scalar(
                out=t3[:, 0:n], in0=t2[:, 0:n], scalar1=-0.5, scalar2=1.5,
                op0=ALU.mult, op1=ALU.add), reads=[t2], writes=[t3])
            p.op("v", lambda v, s_=src, d_=dst: v.tensor_tensor(
                out=d_[:, 0:n], in0=s_[:, 0:n], in1=t3[:, 0:n], op=ALU.mult),
                reads=[src, t3], writes=[dst])
            src, dst = dst, src
        o = T[out_name]
        p.op("v", lambda v, s_=src: v.tensor_copy(out=o[:, 0:n], in_=s_[:, 0:n]),
             reads=[src], writes=[o])

    def rsqrt_newton_w(src_tok, src_ap, out_tok, out_ap, scr, P, n, iters=2):
        (t1k, t1), (t2k, t2), (t3k, t3), (sAk, sA), (sBk, sB) = scr
        p.op("v", lambda v: v.tensor_scalar(
            out=t1.bitcast(I32), in0=src_ap.bitcast(I32), scalar1=1,
            scalar2=None, op0=ALU.arith_shift_right), reads=[src_tok], writes=[t1k])
        p.op("v", lambda v: v.tensor_tensor(
            out=sA.bitcast(I32),
            in0=T["qkcol"][:][0:P, :].broadcast_to([P, n]),
            in1=t1.bitcast(I32), op=ALU.subtract),
            reads=[t1k, T["qkcol"]], writes=[sAk])
        src, dst = (sAk, sA), (sBk, sB)
        for _ in range(iters):
            p.op("v", lambda v, s_=src[1]: v.tensor_tensor(
                out=t1, in0=s_, in1=s_, op=ALU.mult),
                reads=[src[0]], writes=[t1k])
            p.op("v", lambda v: v.tensor_tensor(
                out=t2, in0=t1, in1=src_ap, op=ALU.mult),
                reads=[t1k, src_tok], writes=[t2k])
            p.op("v", lambda v: v.tensor_scalar(
                out=t3, in0=t2, scalar1=-0.5, scalar2=1.5,
                op0=ALU.mult, op1=ALU.add), reads=[t2k], writes=[t3k])
            p.op("v", lambda v, s_=src[1], d_=dst[1]: v.tensor_tensor(
                out=d_, in0=s_, in1=t3, op=ALU.mult),
                reads=[src[0], t3k], writes=[dst[0]])
            src, dst = dst, src
        p.op("v", lambda v, s_=src[1]: v.tensor_copy(out=out_ap, in_=s_),
             reads=[src[0]], writes=[out_tok])

    def transpose8(src_tok, src_ap_of, dst_tok, dst_ap_of, evac_eng="s"):
        for half in range(2):
            ps = PS[4 + half]
            def mk(t_, half=half, ps=ps):
                last = None
                for j in range(4):
                    last = t_.matmul(out=ps[:, j * 128:(j + 1) * 128],
                                     lhsT=src_ap_of(half * 4 + j), rhs=T["ident"][:],
                                     start=True, stop=True)
                return last
            p.op("t", mk, reads=[src_tok, T["ident"]], writes=[ps])
            out_ap = dst_ap_of(half)
            p.op(evac_eng, lambda e, o=out_ap, ps=ps: (
                e.copy(out=o, in_=ps[:].rearrange("p (c n) -> p c n", n=128))
                if evac_eng == "s" else
                e.tensor_copy(out=o, in_=ps[:].rearrange("p (c n) -> p c n", n=128))),
                reads=[ps], writes=[dst_tok])

    def quant_pipeline(src_tok, src_ap):
        s3 = src_ap.rearrange("p (b k) -> p b k", k=32)
        y2, c2, c4, t1 = T["qb0"], T["qb1"], T["qb2"], T["qb3"]
        p.op("v", lambda v: v.tensor_reduce(out=T["bmax"][:], in_=s3, axis=AX.X,
                                            op=ALU.max, apply_absolute_value=True),
             reads=[src_tok], writes=[T["bmax"]])
        p.op("v", lambda v: v.tensor_scalar(out=T["qscale"][:], in0=T["bmax"][:],
                                            scalar1=INV75, scalar2=1e-30,
                                            op0=ALU.mult, op1=ALU.max),
             reads=[T["bmax"]], writes=[T["qscale"]])
        p.op("v", lambda v: v.reciprocal(out=T["qrscale"][:], in_=T["qscale"][:]),
             reads=[T["qscale"]], writes=[T["qrscale"]])
        p.op("v", lambda v: v.tensor_tensor(
            out=T["qy"][:].rearrange("p (b k) -> p b k", k=32), in0=s3,
            in1=T["qrscale"][:].unsqueeze(2).broadcast_to([128, 32, 32]), op=ALU.mult),
            reads=[src_tok, T["qrscale"]], writes=[T["qy"]])
        p.op("s", lambda s_: s_.activation(out=y2[:], in_=T["qy"][:], func=AF.Square),
             reads=[T["qy"]], writes=[y2])
        p.op("v", lambda g: g.tensor_scalar(out=c2[:], in0=y2[:], scalar1=4.0,
                                            scalar2=None, op0=ALU.is_ge),
             reads=[y2], writes=[c2])
        p.op("v", lambda g: g.tensor_scalar(out=c4[:], in0=y2[:], scalar1=16.0,
                                            scalar2=None, op0=ALU.is_ge),
             reads=[y2], writes=[c4])
        p.op("v", lambda g: g.scalar_tensor_tensor(out=t1[:], in0=c2[:], scalar=2.0,
                                                   in1=c4[:], op0=ALU.mult, op1=ALU.add),
             reads=[c2, c4], writes=[t1])
        p.op("v", lambda g: g.tensor_scalar(out=y2[:], in0=t1[:], scalar1=-2.0,
                                            scalar2=8.0, op0=ALU.mult, op1=ALU.add),
             reads=[t1], writes=[y2])
        p.op("v", lambda g: g.scalar_tensor_tensor(out=t1[:], in0=c4[:], scalar=2.0,
                                                   in1=c2[:], op0=ALU.mult, op1=ALU.add),
             reads=[c4, c2], writes=[t1])
        p.op("v", lambda g: g.tensor_scalar(out=c2[:], in0=t1[:], scalar1=0.125,
                                            scalar2=0.125, op0=ALU.mult, op1=ALU.add),
             reads=[t1], writes=[c2])
        p.op("v", lambda v: v.tensor_tensor(out=T["qtt"][:], in0=T["qy"][:], in1=y2[:],
                                            op=ALU.mult),
             reads=[T["qy"], y2], writes=[T["qtt"]])
        p.op("v", lambda v: v.tensor_scalar(out=T["qk"][:], in0=T["qtt"][:],
                                            scalar1=MAGIC, scalar2=MAGIC,
                                            op0=ALU.add, op1=ALU.subtract),
             reads=[T["qtt"]], writes=[T["qk"]])
        p.op("v", lambda g: g.tensor_tensor(out=c4[:], in0=T["qk"][:], in1=c2[:],
                                            op=ALU.mult),
             reads=[T["qk"], c2], writes=[c4])
        p.op("v", lambda v: v.tensor_tensor(
            out=T["xdq"][:].rearrange("p (b k) -> p b k", k=32),
            in0=c4[:].rearrange("p (b k) -> p b k", k=32),
            in1=T["qscale"][:].unsqueeze(2).broadcast_to([128, 32, 32]), op=ALU.mult),
            reads=[c4, T["qscale"]], writes=[T["xdq"]])

    p.op("g", lambda g: g.memset(T["ident"][:], 0.0), writes=[T["ident"]])
    p.op("g", lambda g: g.affine_select(
        out=T["ident"][:], in_=T["ident"][:], compare_op=ALU.not_equal, fill=1.0,
        base=0, pattern=[[-1, 128]], channel_multiplier=1),
        reads=[T["ident"]], writes=[T["ident"]])
    p.op("v", lambda v: v.memset(T["onesr"][:], 1.0), writes=[T["onesr"]])
    p.op("v", lambda v: v.memset(T["qkcol"][:], QK_RSQRT), writes=[T["qkcol"]])
    p.op("v", lambda v: v.memset(T["gm"][:], 0.0), writes=[T["gm"]])

    p.dma("y", T["colp"][:], colpack[:, :], writes=[T["colp"]])
    p.dma("s", T["cstb"][:], cstpack[:, :], writes=[T["cstb"]])
    p.dma("y", T["xtile"][:, 0:128], wphibd[:, :], writes=[XT[0]])
    p.op("v", lambda v: v.tensor_copy(out=T["wphib"][:], in_=T["xtile"][:, 0:128]),
         reads=[XT[0]], writes=[T["wphib"]])
    for i, nm in ((0, "bzrow"), (1, "bqrow"), (2, "brrow"), (3, "bhrow"),
                  (4, "borow"), (5, "bfrow")):
        p.dma("y", T["xtile"][0:1, 0:D], rowpack[i:i + 1, :], writes=[XT[0]])
        p.op("v", lambda v, nm=nm: v.tensor_copy(
            out=T[nm][:], in_=T["xtile"][0:1, 0:D]),
            reads=[XT[0]], writes=[T[nm]])
    p.op("v", lambda v: v.reciprocal(out=T["rgcol"][:], in_=T["colp"][:, 0:8]),
         reads=[T["colp"]], writes=[T["rgcol"]])
    p.op("v", lambda v: v.tensor_tensor(out=T["cvcol"][:], in0=T["colp"][:, 8:16],
                                        in1=T["rgcol"][:], op=ALU.mult),
         reads=[T["colp"], T["rgcol"]], writes=[T["cvcol"]])
    p.op("v", lambda v: v.tensor_copy(out=T["cvcolb"][:], in_=T["cvcol"][:]),
         reads=[T["cvcol"]], writes=[T["cvcolb"]])
    p.op("v", lambda v: v.memset(T["pihalf"][:], PI / 2), writes=[T["pihalf"]])

    def load_weight(nm, dst, scale_by_g):
        for k in range(NCH):
            stg = T["xtile"]
            sl = k % 3
            off = sl * D
            p.dma(("y", "s", "g")[sl], stg[:, off:off + D],
                  w_in[nm][k * 128:(k + 1) * 128, :], writes=[XT[sl]])
            if scale_by_g:
                p.op("v", lambda v, k=k, off=off: v.tensor_scalar(
                    out=T[dst][:, k * D:(k + 1) * D], in0=stg[:, off:off + D],
                    scalar1=T["colp"][:, k:k + 1], scalar2=None, op0=ALU.mult),
                    reads=[XT[sl], T["colp"]], writes=[T[dst]])
            else:
                p.op("v", lambda v, k=k, off=off: v.tensor_copy(
                    out=T[dst][:, k * D:(k + 1) * D], in_=stg[:, off:off + D]),
                    reads=[XT[sl]], writes=[T[dst]])

    load_weight("wz", "wzb", False)
    load_weight("wq", "wqb", True)
    load_weight("wr", "wrb", True)
    load_weight("wh", "whb", True)

    def fold_bias(wb, rowname):
        for half in range(2):
            ps = PS[half]
            def mk(t_, half=half, ps=ps):
                last = None
                for k in range(NCH):
                    last = t_.matmul(
                        out=ps[0:1, 0:512], lhsT=T["cvcolb"][:, k:k + 1],
                        rhs=T[wb][:, k * D + half * 512: k * D + half * 512 + 512],
                        start=(k == 0), stop=(k == NCH - 1))
                return last
            p.op("t", mk, reads=[T["cvcolb"], T[wb]], writes=[ps])
        for half in range(2):
            ps = PS[half]
            p.op("v", lambda v, half=half, ps=ps: v.tensor_tensor(
                out=T["foldrow"][:][0:1, half * 512:half * 512 + 512],
                in0=ps[0:1, 0:512],
                in1=T[rowname][:][:, half * 512:half * 512 + 512],
                op=ALU.add), reads=[ps, T[rowname]], writes=[T["foldrow"]])
        p.op("v", lambda v: v.tensor_copy(out=T[rowname][:], in_=T["foldrow"][:][0:1, :]),
             reads=[T["foldrow"]], writes=[T[rowname]])

    fold_bias("wqb", "bqrow")
    fold_bias("wrb", "brrow")
    fold_bias("whb", "bhrow")

    def bias_cols(rowname, colname):
        ps = PS[2]
        def mk(t_):
            last = None
            for c in range(NCH):
                last = t_.matmul(out=ps[:, c:c + 1],
                                 lhsT=T[rowname][:][0:1, c * 128:(c + 1) * 128],
                                 rhs=T["onesr"][:][0:1, 0:1], start=True, stop=True)
            return last
        p.op("t", mk, reads=[T[rowname], T["onesr"]], writes=[ps])
        p.op("v", lambda v: v.tensor_copy(out=T[colname][:], in_=ps[:, 0:8]),
             reads=[ps], writes=[T[colname]])

    def bias_bcast(rowname, dsts):
        for half in range(2):
            ps = PS[half]
            p.op("t", lambda t_, half=half, ps=ps: t_.matmul(
                out=ps[:, 0:512], lhsT=T["onesr"][:],
                rhs=T[rowname][:][:, half * 512:half * 512 + 512],
                start=True, stop=True), reads=[T["onesr"], T[rowname]], writes=[ps])
            dtok, dap = dsts[half]
            p.op("v", lambda v, ps=ps, dap=dap: v.tensor_copy(
                out=dap, in_=ps[:, 0:512]), reads=[ps], writes=[dtok])

    bias_cols("bqrow", "bqcol")
    bias_cols("brrow", "brcol")
    bias_bcast("bhrow", [(T["bhb"], T["bhb"][:, 0:512]),
                         (T["bhb"], T["bhb"][:, 512:1024])])

    p.checkpoint()
    for t in range(NT):
        t0 = t * TT
        xt = T["xtile"]
        for sub in range(NSUB):
            p.dma("y", xt[:, sub * D:(sub + 1) * D],
                  xq[t0 + sub * SB: t0 + (sub + 1) * SB, :], writes=[XT[sub]])
        for sub in range(NSUB):
            p.op("v", lambda v, sub=sub: v.bn_stats(
                out=T["bn6"][:, 0:6], in_=xt[:, sub * D: sub * D + 512]),
                reads=[XT[sub]], writes=[T["bn6"]])
            p.op("v", lambda v, sub=sub: v.bn_stats(
                out=T["bn6"][:, 6:12], in_=xt[:, sub * D + 512: sub * D + 1024]),
                reads=[XT[sub]], writes=[T["bn6"]])
            p.op("v", lambda v, sub=sub: v.bn_aggr(
                out=T["xst"][:, 2 * sub:2 * sub + 2], in_=T["bn6"][:]),
                reads=[T["bn6"]], writes=[T["xst"]])
        p.op("v", lambda v: v.tensor_scalar(out=T["stat8"][:], in0=T["xst"][:],
                                            scalar1=1e-5, scalar2=None, op0=ALU.add),
             reads=[T["xst"]], writes=[T["stat8"]])
        rsqrt_newton("stat8", 8, "rstd1")
        for sub in range(NSUB):
            p.op("v", lambda v, sub=sub: v.scalar_tensor_tensor(
                out=T["nm1"][:, sub:sub + 1], in0=T["xst"][:, 2 * sub:2 * sub + 1],
                scalar=-1.0, in1=T["rstd1"][:, 2 * sub + 1:2 * sub + 2],
                op0=ALU.mult, op1=ALU.mult),
                reads=[T["xst"], T["rstd1"]], writes=[T["nm1"]])

        for sub in range(NSUB):
            p.op("v", lambda v, sub=sub: v.tensor_scalar(
                out=T["xn"][:], in0=xt[:, sub * D:(sub + 1) * D],
                scalar1=T["rstd1"][:, 2 * sub + 1:2 * sub + 2],
                scalar2=T["nm1"][:, sub:sub + 1], op0=ALU.mult, op1=ALU.add),
                reads=[XT[sub], T["nm1"], T["rstd1"]], writes=[T["xn"]])
            p.op("s", lambda s_: s_.copy(out=T["xnb"][:], in_=T["xn"][:]),
                 reads=[T["xn"]], writes=[T["xnb"]])
            xnT3 = T["xnT"][:].rearrange("p (c n) -> p c n", n=TT)
            transpose8(T["xnb"], lambda c: T["xnb"][:, c * 128:(c + 1) * 128],
                       T["xnT"],
                       lambda half, sub=sub: xnT3[:, half * 4:half * 4 + 4,
                                                  sub * 128:(sub + 1) * 128])
            for half in range(2):
                ps = PS[half]
                def mkz(t_, half=half, sub=sub, ps=ps):
                    last = t_.matmul(out=ps[:, 0:512], lhsT=T["onesr"][:],
                                     rhs=T["bzrow"][:][:, half * 512:half * 512 + 512],
                                     start=True, stop=False)
                    for k in range(NCH):
                        last = t_.matmul(
                            out=ps[:, 0:512],
                            lhsT=T["xnT"][:, k * TT + sub * 128: k * TT + (sub + 1) * 128],
                            rhs=T["wzb"][:, k * D + half * 512: k * D + half * 512 + 512],
                            start=False, stop=(k == NCH - 1))
                    return last
                p.op("t", mkz, reads=[T["onesr"], T["bzrow"], T["xnT"], T["wzb"]],
                     writes=[ps])
                p.op("v", lambda v, half=half, ps=ps: v.bn_stats(
                    out=T["bn6"][:, 6 * half:6 * half + 6], in_=ps[:, 0:512]),
                    reads=[ps], writes=[T["bn6"]])
            p.op("v", lambda v: v.bn_aggr(out=T["zst"][:], in_=T["bn6"][:]),
                 reads=[T["bn6"]], writes=[T["zst"]])
            p.op("v", lambda v: v.tensor_scalar(out=T["stat2"][:], in0=T["zst"][:],
                                                scalar1=1e-5, scalar2=None, op0=ALU.add),
                 reads=[T["zst"]], writes=[T["stat2"]])
            rsqrt_newton("stat2", 2, "rstd2")
            p.op("v", lambda v: v.scalar_tensor_tensor(
                out=T["nm2"][:], in0=T["zst"][:, 0:1], scalar=-1.0,
                in1=T["rstd2"][:, 1:2], op0=ALU.mult, op1=ALU.mult),
                reads=[T["zst"], T["rstd2"]], writes=[T["nm2"]])
            ubuf = T[f"ub{sub % 2}"]
            for half in range(2):
                ps = PS[half]
                p.op("v", lambda v, half=half, ubuf=ubuf, ps=ps: v.tensor_scalar(
                    out=ubuf[:, half * 512: half * 512 + 512],
                    in0=ps[:, 0:512], scalar1=T["rstd2"][:, 1:2],
                    scalar2=T["nm2"][:], op0=ALU.mult, op1=ALU.add),
                    reads=[ps, T["nm2"], T["rstd2"]], writes=[ubuf])
            uT3 = T["uT"][:].rearrange("p (c n) -> p c n", n=TT)
            transpose8(ubuf, lambda c, ubuf=ubuf: ubuf[:, c * 128:(c + 1) * 128],
                       T["uT"],
                       lambda half, sub=sub: uT3[:, half * 4:half * 4 + 4,
                                                 sub * 128:(sub + 1) * 128])
            quant_pipeline(T["xn"], T["xn"][:])
            xdqs = T[f"xdqs{sub % 2}"]
            xdqs3 = xdqs[:].rearrange("p (c n) -> p c n", n=128)
            transpose8(T["xdq"], lambda c: T["xdq"][:, c * 128:(c + 1) * 128],
                       xdqs,
                       lambda half: xdqs3[:, half * 4:half * 4 + 4, :])
            for c in range(NCH):
                p.dma("g", xdq_hbm[c, :, t0 + sub * SB: t0 + (sub + 1) * SB],
                      xdqs[:, c * 128:(c + 1) * 128], reads=[xdqs], writes=[xdq_hbm])

        for sub in range(NSUB):
            for half in range(2):
                ps = PS[half]
                def mkh(t_, half=half, sub=sub, ps=ps):
                    last = None
                    for k in range(NCH):
                        last = t_.matmul(
                            out=ps[:, 0:512],
                            lhsT=T["uT"][:, k * TT + sub * 128: k * TT + (sub + 1) * 128],
                            rhs=T["whb"][:, k * D + half * 512: k * D + half * 512 + 512],
                            start=(k == 0), stop=(k == NCH - 1))
                    return last
                p.op("t", mkh, reads=[T["uT"], T["whb"]], writes=[ps])
                base0 = sub * H * 192 + half * 8 * 192
                src3 = ps[:, 0:512].rearrange("p (h x) -> p h x", x=64)
                bhb3 = T["bhb"][:, half * 512:half * 512 + 512].rearrange(
                    "p (h x) -> p h x", x=64)
                dstt = T["rchv"][:, base0: base0 + 192 * 8]
                dst3 = dstt.rearrange("p (h x) -> p h x", x=192)[:, :, 128:192]
                p.op("v", lambda v, dst3=dst3, src3=src3, bhb3=bhb3: v.tensor_tensor(
                    out=dst3, in0=src3, in1=bhb3, op=ALU.add),
                    reads=[ps, T["bhb"]], writes=[T["rchv"]])

        for c in range(NCH):
            bi = c % 2
            qstage, rstage = T[f"qTs{bi}"], T[f"rTs{bi}"]
            kcs, psis = T[f"kcs{bi}"], T[f"psis{bi}"]
            sinb, cosb = T[f"sinb{bi}"], T[f"cosb{bi}"]
            for (wname, bcol, stage) in (("wqb", "bqcol", qstage), ("wrb", "brcol", rstage)):
                ps = PS[bi]
                def mkq(t_, wname=wname, c=c, ps=ps):
                    last = None
                    for k in range(NCH):
                        last = t_.matmul(
                            out=ps[:, 0:512],
                            lhsT=T[wname][:, k * D + c * 128: k * D + (c + 1) * 128],
                            rhs=T["uT"][:, k * TT:(k + 1) * TT],
                            start=(k == 0), stop=(k == NCH - 1))
                    return last
                p.op("t", mkq, reads=[T[wname], T["uT"]], writes=[ps])
                p.op("v", lambda v, ps=ps, stage=stage, bcol=bcol, c=c: v.tensor_scalar(
                    out=stage[:], in0=ps[:, 0:512], scalar1=T[bcol][:, c:c + 1],
                    scalar2=None, op0=ALU.add),
                    reads=[ps, T[bcol]], writes=[stage])
            phb = T["phb0"]
            for side, (stage, pbuf, sbuf_) in enumerate(
                    ((qstage, T[f"pqs{bi}"], T[f"sqs{bi}"]),
                     (rstage, T[f"prs{bi}"], T[f"srs{bi}"]))):
                ps = PS[2 + bi]
                p.op("t", lambda t_, stage=stage, ps=ps: t_.matmul(
                    out=ps[:, 0:512], lhsT=T["wphib"][:], rhs=stage[:],
                    start=True, stop=True),
                    reads=[T["wphib"], stage], writes=[ps])
                p.op("v", lambda v, ps=ps, phb=phb: v.tensor_scalar(
                    out=phb[:], in0=ps[:, 0:512], scalar1=T["colp"][:, 16:17],
                    scalar2=None, op0=ALU.add),
                    reads=[ps, T["colp"]], writes=[phb])
                p.op("v", lambda g, phb=phb, kcs=kcs: g.tensor_scalar(
                    out=kcs[:], in0=phb[:], scalar1=PI / 2, scalar2=None,
                    op0=ALU.is_ge),
                    reads=[phb], writes=[kcs])
                p.op("v", lambda v, phb=phb, kcs=kcs, psis=psis: v.scalar_tensor_tensor(
                    out=psis[:], in0=kcs[:], scalar=-2 * PI, in1=phb[:],
                    op0=ALU.mult, op1=ALU.add),
                    reads=[kcs, phb], writes=[psis])
                p.op("s", lambda s_, phb=phb, sinb=sinb: s_.activation(
                    out=sinb[:], in_=phb[:], func=AF.Sin),
                    reads=[phb], writes=[sinb])
                p.op("s", lambda s_, psis=psis, cosb=cosb: s_.activation(
                    out=cosb[:], in_=psis[:], func=AF.Sin, bias=T["pihalf"][:]),
                    reads=[psis, T["pihalf"]], writes=[cosb])
                eng = "v" if side == 0 else "g"
                p.op(eng, lambda e, stage=stage, cosb=cosb, pbuf=pbuf: e.tensor_tensor(
                    out=pbuf[:], in0=stage[:], in1=cosb[:], op=ALU.mult),
                    reads=[stage, cosb], writes=[pbuf])
                p.op(eng, lambda e, stage=stage, sinb=sinb, sbuf_=sbuf_: e.tensor_tensor(
                    out=sbuf_[:], in0=stage[:], in1=sinb[:], op=ALU.mult),
                    reads=[stage, sinb], writes=[sbuf_])
            pq, sq = T[f"pqs{bi}"], T[f"sqs{bi}"]
            for (buf, r0) in ((pq, 0), (sq, 64)):
                p.dma("g", qc_hbm[2 * c, r0:r0 + 64, t0:t0 + TT], buf[0:64, :],
                      reads=[buf], writes=[qc_hbm])
                p.dma("g", qc_hbm[2 * c + 1, r0:r0 + 64, t0:t0 + TT], buf[64:128, :],
                      reads=[buf], writes=[qc_hbm])
            pr, sr = T[f"prs{bi}"], T[f"srs{bi}"]
            for bidx, (buf, col0) in enumerate(((pr, 0), (sr, 64))):
                ps = PS[4 + bidx]
                def mkt4(t_, buf=buf, ps=ps):
                    last = None
                    for sub in range(NSUB):
                        last = t_.matmul(
                            out=ps[:, sub * 128:(sub + 1) * 128],
                            lhsT=buf[:, sub * 128:(sub + 1) * 128],
                            rhs=T["ident"][:], start=True, stop=True)
                    return last
                p.op("t", mkt4, reads=[buf, T["ident"]], writes=[ps])
                src4 = ps[:].rearrange("p (s h x) -> p s h x", s=NSUB, h=2)
                base = (2 * c) * 192 + col0
                dst4 = T["rchv"][:, base: base + (NSUB - 1) * H * 192 + 192 + 64].rearrange(
                    "p (rest) -> p rest") if False else None
                dst4 = T["rchv"][:].rearrange(
                    "p (s hh x) -> p s hh x", s=NSUB, x=192)[
                    :, :, 2 * c:2 * c + 2, col0:col0 + 64]
                p.op("s", lambda s_, src4=src4, dst4=dst4: s_.copy(
                    out=dst4, in_=src4),
                    reads=[ps], writes=[T["rchv"]])
            pg = PG[c % 2]
            def mkgm(t_, c=c, pg=pg):
                last = None
                for hh in range(2):
                    h = 2 * c + hh
                    for sub in range(NSUB):
                        base = sub * H * 192 + h * 192
                        last = t_.matmul(
                            out=pg[:, hh * 192:(hh + 1) * 192],
                            lhsT=T["rchv"][:, base:base + 128],
                            rhs=T["rchv"][:, base:base + 192],
                            start=(sub == 0), stop=(sub == NSUB - 1))
                return last
            p.op("t", mkgm, reads=[T["rchv"]], writes=[pg])
            p.op("v", lambda v, c=c, pg=pg: v.tensor_tensor(
                out=T["gm"][:, c * 384:(c + 1) * 384],
                in0=T["gm"][:, c * 384:(c + 1) * 384], in1=pg[:, 0:384], op=ALU.add),
                reads=[pg, T["gm"]], writes=[T["gm"]])


        p.checkpoint()
    p.op("s", lambda s_: s_.copy(out=T["gmb"][:], in_=T["gm"][:]),
         reads=[T["gm"]], writes=[T["gmb"]])
    bias_bcast("bfrow", [(T["qTs0"], T["qTs0"][:]), (T["qTs1"], T["qTs1"][:])])
    bias_bcast("borow", [(T["rTs0"], T["rTs0"][:]), (T["rTs1"], T["rTs1"][:])])
    load_weight("wo", "wob", False)
    for c in range(NCH):
        stg = T["xtile"]
        off = (c % 2) * D
        p.dma(("y", "s")[c % 2], stg[:, off:off + D],
              w_in["wf"][c * 128:(c + 1) * 128, :], writes=[XT[c % 2]])
        quant_pipeline(XT[c % 2], stg[:, off:off + D])
        wdqT3 = T["wdqT"][:].rearrange("p (k n) -> p k n", n=D)
        transpose8(T["xdq"], lambda k: T["xdq"][:, k * 128:(k + 1) * 128],
                   T["wdqT"],
                   lambda half, c=c: wdqT3[:, half * 4:half * 4 + 4,
                                           c * 128:(c + 1) * 128])

    p.checkpoint()
    GS = [(T["gsb0"][:], T["gsb0"]), (T["gsb1"][:], T["gsb1"]),
          (T["xdqs0"][:][:, 0:D], T["xdqs0"]), (T["xdqs1"][:][:, 0:D], T["xdqs1"])]
    SS = [(T["ssb0"][:], T["ssb0"]), (T["ssb1"][:], T["ssb1"])]
    PSB = [(T["psb0"][:], T["psb0"]), (T["psb1"][:], T["psb1"])]
    XNS = [(T["xn"][:], T["xn"]), (T["gm"][:][:, 0:D], T["gm"])]
    for t in range(NT):
        t0 = t * TT
        if t % 2 == 0:
            qct_tok, qct = T["qcT"], T["qcT"][:]
            xdt_tok, xdt = T["xdqT2"], T["xdqT2"][:]
        else:
            qct_tok, qct = T["rchv"], T["rchv"][:][:, 0:H * TT]
            xdt_tok, xdt = T["rchv"], T["rchv"][:][:, H * TT:H * TT + NCH * TT]
        for h in range(H):
            p.dma(("y", "g")[h % 2], qct[:, h * TT:(h + 1) * TT],
                  qc_hbm[h, :, t0:t0 + TT], reads=[qc_hbm], writes=[qct_tok])
        for c in range(NCH):
            p.dma(("y", "g")[c % 2], xdt[:, c * TT:(c + 1) * TT],
                  xdq_hbm[c, :, t0:t0 + TT], reads=[xdq_hbm], writes=[xdt_tok])
        for sub in range(NSUB):
            p.dma(("y", "g")[sub % 2], T["xtile"][:, sub * D:(sub + 1) * D],
                  xq[t0 + sub * SB: t0 + (sub + 1) * SB, :], writes=[XT[sub]])
        def emit_A(h):
            pg = PG[h % 2]
            goff = (h // 2) * 384 + (h % 2) * 192
            p.op("t", lambda t_, pg=pg, goff=goff, h=h, qct=qct: t_.matmul(
                out=pg[:, 0:512], lhsT=T["gmb"][:, goff:goff + 128],
                rhs=qct[:, h * TT:(h + 1) * TT], start=True, stop=True,
                skip_group_check=True),
                reads=[T["gmb"], qct_tok], writes=[pg])

        def emit_prod_n2(h):
            pg = PG[h % 2]
            pr = T[f"prb{h % 2}"]
            p.op("v", lambda v, pg=pg, pr=pr, h=h, qct=qct: v.tensor_tensor(
                out=pr[:], in0=pg[:, 0:512], in1=qct[:, h * TT:(h + 1) * TT],
                op=ALU.mult), reads=[pg, qct_tok], writes=[pr])
            p.op("t", lambda t_, pr=pr, h=h: t_.matmul(
                out=PS[3][0:16, 0:512],
                lhsT=T["cstb"][:, 1024 + h * 16:1024 + (h + 1) * 16], rhs=pr[:],
                start=(h == 0), stop=(h == H - 1), skip_group_check=True),
                reads=[T["cstb"], pr, PS[3]], writes=[PS[3]])

        emit_A(0)
        for h in range(1, H):
            emit_A(h)
            emit_prod_n2(h - 1)
        emit_prod_n2(H - 1)

        def ffn_sub(sub):
            xnap, xntok = XNS[sub % 2]
            gsap, gstok = GS[sub]
            for half in range(2):
                ps = PS[1 if half == 0 else 2]
                def mkg(t_, half=half, sub=sub, ps=ps, xdt=xdt):
                    last = None
                    for k in range(NCH):
                        last = t_.matmul(
                            out=ps[:, 0:512],
                            lhsT=xdt[:, k * TT + sub * 128: k * TT + (sub + 1) * 128],
                            rhs=T["wdqT"][:, k * D + half * 512: k * D + half * 512 + 512],
                            start=(k == 0), stop=(k == NCH - 1), skip_group_check=True)
                    return last
                p.op("t", mkg, reads=[xdt_tok, T["wdqT"]], writes=[ps])
                p.op("v", lambda v, ps=ps, half=half, xnap=xnap: v.tensor_tensor(
                    out=xnap[:, half * 512:half * 512 + 512], in0=ps[:, 0:512],
                    in1=T[f"qTs{half}"][:], op=ALU.add),
                    reads=[ps, T[f"qTs{half}"]], writes=[xntok])
                p.op("s", lambda s_, half=half, xnap=xnap, gsap=gsap: s_.activation(
                    out=gsap[:, half * 512:half * 512 + 512],
                    in_=xnap[:, half * 512:half * 512 + 512], func=AF.Silu),
                    reads=[xntok], writes=[gstok])

        ffn_sub(0)
        ffn_sub(1)

        p.op("v", lambda v: v.tensor_scalar(
            out=T["psis0"][0:16, 0:512], in0=PS[3][0:16, 0:512], scalar1=1e-24,
            scalar2=None, op0=ALU.add), reads=[PS[3]], writes=[T["psis0"]])
        scr16 = [(T["gm"], T["gm"][0:16, D + i * 512: D + (i + 1) * 512])
                 for i in range(4)] + [(T["psis1"], T["psis1"][0:16, 0:512])]
        rsqrt_newton_w(T["psis0"], T["psis0"][0:16, 0:512],
                       T["cosb0"], T["cosb0"][0:16, 0:512], scr16, 16, 512)

        for c in range(NCH):
            psR = PS[4 + (c % 2)]
            p.op("t", lambda t_, c=c, psR=psR: t_.matmul(
                out=psR[:, 0:512], lhsT=T["cstb"][0:16, c * 128:(c + 1) * 128],
                rhs=T["cosb0"][0:16, 0:512], start=True, stop=True,
                skip_group_check=True),
                reads=[T["cstb"], T["cosb0"]], writes=[psR])
            rnc = T[f"rnc{c % 2}"]
            p.op("s", lambda s_, psR=psR, rnc=rnc: s_.copy(
                out=rnc[:], in_=psR[:, 0:512]), reads=[psR], writes=[rnc])
            pa = PG[c % 2]
            def mkat(t_, c=c, pa=pa, qct=qct):
                last = None
                for hh in range(2):
                    h = 2 * c + hh
                    goff = (h // 2) * 384 + (h % 2) * 192 + 128
                    last = t_.matmul(out=pa[hh * 64:(hh + 1) * 64, 0:512],
                                     lhsT=T["gmb"][:, goff:goff + 64],
                                     rhs=qct[:, h * TT:(h + 1) * TT],
                                     start=True, stop=True, skip_group_check=True)
                return last
            p.op("t", mkat, reads=[T["gmb"], qct_tok], writes=[pa])
            p.op("v", lambda v, c=c, pa=pa, rnc=rnc: v.tensor_tensor(
                out=T["attT"][:, c * TT:(c + 1) * TT], in0=pa[:, 0:512], in1=rnc[:],
                op=ALU.mult), reads=[pa, rnc], writes=[T["attT"]])

        ffn_sub(2)
        ffn_sub(3)

        for sub in range(NSUB):
            ssap, sstok = SS[sub % 2]
            for half in range(2):
                ps = PS[5 if half == 0 else 0]
                def mks(t_, half=half, sub=sub, ps=ps):
                    last = None
                    for k in range(NCH):
                        last = t_.matmul(
                            out=ps[:, 0:512],
                            lhsT=T["attT"][:, k * TT + sub * 128: k * TT + (sub + 1) * 128],
                            rhs=T["wob"][:, k * D + half * 512: k * D + half * 512 + 512],
                            start=(k == 0), stop=(k == NCH - 1), skip_group_check=True)
                    return last
                p.op("t", mks, reads=[T["attT"], T["wob"]], writes=[ps])
                p.op("v", lambda v, ps=ps, half=half, ssap=ssap: v.tensor_tensor(
                    out=ssap[:, half * 512:half * 512 + 512], in0=ps[:, 0:512],
                    in1=T[f"rTs{half}"][:], op=ALU.add),
                    reads=[ps, T[f"rTs{half}"]], writes=[sstok])
            outb = T[f"outs{sub % 2}"]
            psap, pstok = PSB[sub % 2]
            gsap, gstok = GS[sub]
            p.op("v", lambda v, ssap=ssap, gsap=gsap, psap=psap: v.tensor_tensor(
                out=psap[:], in0=ssap[:], in1=gsap[:], op=ALU.mult),
                reads=[sstok, gstok], writes=[pstok])
            p.op("v", lambda v, sub=sub, outb=outb, psap=psap: v.tensor_tensor(
                out=outb[:], in0=T["xtile"][:, sub * D:(sub + 1) * D], in1=psap[:],
                op=ALU.add),
                reads=[XT[sub], pstok], writes=[outb])
            p.dma("s", out_d[t0 + sub * SB: t0 + (sub + 1) * SB, :], outb[:],
                  reads=[outb])

        p.checkpoint()

    import os
    lim = os.environ.get("KLIMIT")
    if lim is not None:
        p.truncate(int(lim))
    p.final_store_barrier()

    with nc.Block() as block:
        p.run(block)
    es.close()
    return nc


_NC_CACHE = None


def kernel(**inputs):
    global _NC_CACHE
    x = np.ascontiguousarray(np.asarray(inputs["L0_attn_query"], dtype=np.float32))
    w = {"wz": inputs["L0_attn_Wz"], "wq": inputs["L0_attn_Wq"],
         "wr": inputs["L0_attn_Wr"], "wh": inputs["L0_attn_Wh"],
         "wo": inputs["L0_attn_Wo"], "wf": inputs["L0_ffn_weight"]}
    w = {k: np.ascontiguousarray(np.asarray(v, dtype=np.float32)) for k, v in w.items()}
    wphi = np.asarray(inputs["L0_attn_Wphi"], dtype=np.float32)
    wphibd = np.zeros((128, 128), dtype=np.float32)
    wphibd[0:64, 0:64] = wphi
    wphibd[64:128, 64:128] = wphi
    colpack = np.zeros((128, 17), dtype=np.float32)
    colpack[:, 0:8] = np.asarray(inputs["L0_attn_ln_g"], np.float32).reshape(8, 128).T
    colpack[:, 8:16] = np.asarray(inputs["L0_attn_ln_b"], np.float32).reshape(8, 128).T
    colpack[0:64, 16] = np.asarray(inputs["L0_attn_bphi"], np.float32)
    colpack[64:128, 16] = np.asarray(inputs["L0_attn_bphi"], np.float32)
    cstpack = np.zeros((128, 1280), dtype=bfloat16)
    hh = np.arange(1024) // 64
    cstpack[0:16, 0:1024] = (hh[None, :] == np.arange(16)[:, None]).astype(bfloat16)
    jj = np.arange(256)
    cstpack[:, 1024:1280] = np.broadcast_to(
        ((jj % 16) == (jj // 16)).astype(bfloat16)[None, :], (128, 256))
    rowpack = np.ascontiguousarray(np.stack([
        np.asarray(inputs["L0_attn_bz"], np.float32),
        np.asarray(inputs["L0_attn_bq"], np.float32),
        np.asarray(inputs["L0_attn_br"], np.float32),
        np.asarray(inputs["L0_attn_bh"], np.float32),
        np.asarray(inputs["L0_attn_bo"], np.float32),
        np.asarray(inputs["L0_ffn_bias"], np.float32)]))
    if _NC_CACHE is None:
        _NC_CACHE = build_program()
    in_maps = []
    for b in range(N_CORES):
        m = {"xq": np.ascontiguousarray(x[b]), "wphibd": wphibd,
             "colpack": colpack, "rowpack": rowpack, "cstpack": cstpack}
        m.update(w)
        in_maps.append(m)
    res = run_bass_kernel_spmd(_NC_CACHE, in_maps, core_ids=list(range(N_CORES)))
    return np.stack([r["out"] for r in res.results]).astype(np.float32)


if __name__ == "__main__":
    pass
